# version 23 (final)
# baseline (speedup 1.0000x reference)
import math
import numpy as np
import ml_dtypes
import concourse.bass as bass
import concourse.mybir as mybir
from concourse.bass_utils import run_bass_kernel_spmd
from contextlib import ExitStack

F32 = mybir.dt.float32
BF16 = mybir.dt.bfloat16
I32 = mybir.dt.int32
U16 = mybir.dt.uint16
AF = mybir.ActivationFunctionType
ALU = mybir.AluOpType
AXX = mybir.AxisListType.X

B = 2
L = 2048
E = 1024
H = 16
D = 64
FA = 4608
FI = 2816
CH = 512
NT = 4
NCH = L // CH
NE = E // 128
NFA = FA // 128
NFI = FI // 128

ENGS = ("sync", "gpsimd", "vector", "scalar", "tensor")

DEBUG_TAPS = ()


class Semw:
    __slots__ = ("h", "n", "name")

    def __init__(self, h, name):
        self.h = h
        self.n = 0
        self.name = name


class Sched:
    def __init__(self):
        self.q = {e: [] for e in ENGS}

    DMA_SEMS = ()

    def op(self, eng, fn, waits=(), inc=None, amt=1):
        w = tuple((s, (s.n if s in Sched.DMA_SEMS else v)) for (s, v) in waits)
        self.q[eng].append((fn, w, inc, amt))
        if inc is not None:
            inc.n += amt
            return inc.n
        return None

    def dma(self, eng, out, in_, sem, waits=(), **kw):
        return self.op(eng, lambda e: e.dma_start(out=out, in_=in_, **kw),
                       waits=waits, inc=sem, amt=16)

    def dmat(self, eng, out, in_, sem, waits=()):
        return self.op(eng, lambda e: e.dma_start(out=out, in_=in_, transpose=True),
                       waits=waits, inc=sem, amt=16)


def _pm(h):
    return [2 * i for i in range(32)] + [2 * i + 1 for i in range(32)]


def host_constants():
    pos = np.arange(L, dtype=np.float64)
    inv = 1.0 / (10000.0 ** (np.arange(0, D, 2, dtype=np.float64) / D))
    fr = pos[:, None] * inv[None, :]
    emb = np.concatenate([fr, fr], axis=1)
    cos = np.cos(emb)
    sin = np.sin(emb)
    ce = cos[:, 0::2].astype(np.float32)
    co = cos[:, 1::2].astype(np.float32)
    se = sin[:, 0::2].astype(np.float32)
    so = sin[:, 1::2].astype(np.float32)
    tri = (np.arange(128)[:, None] <= np.arange(128)[None, :]).astype(np.float32)
    iota_col = np.arange(FA, dtype=np.float32).reshape(NFA, 128).T.copy()
    ident = np.eye(128, dtype=np.float32)
    return (ce, co, se, so, tri.astype(ml_dtypes.bfloat16), iota_col,
            ident.astype(ml_dtypes.bfloat16))


def build_program(taps=()):
    nc = bass.Bass("TRN2", target_bir_lowering=False,
                   detect_race_conditions=False)
    S = Sched()

    def din(name, shape, dt=F32):
        return nc.dram_tensor(name, shape, dt, kind="ExternalInput").ap()

    x_in = din("x", [B, L, E])
    wqkv_in = din("wqkv", [E, 3 * E])
    wout_in = din("wout", [E, E])
    g1_in = din("g1", [1, E])
    g2_in = din("g2", [1, E])
    wdp1_in = din("wdp1", [E, 256])
    wdp2_in = din("wdp2", [1, 256])
    wga_in = din("wga", [E, FA])
    wua_in = din("wua", [E, FA])
    wda_in = din("wda", [FA, E])
    gh_in = din("gh", [1, FA])
    wqg_in = din("wqg", [FI, E])
    wqu_in = din("wqu", [FI, E])
    wqd_in = din("wqd", [E, FI])
    tab_in = din("tabs", [4 * L, 32])
    trim_in = din("trim", [128, 128], BF16)
    iota_in = din("iotac", [128, NFA])
    ident_in = din("ident", [128, 128], BF16)

    out_d = nc.dram_tensor("out", [B, L, E], F32, kind="ExternalOutput").ap()

    tap_d = {}
    for t in taps:
        nm, shape, dt = t
        tap_d[nm] = nc.dram_tensor("tap_" + nm, shape, dt, kind="ExternalOutput").ap()

    def dscr(name, shape, dt):
        return nc.dram_tensor(name, shape, dt, kind="Internal").ap()

    h2_s = dscr("h2_s", [B, L, E], BF16)
    x1a_s = dscr("x1a_s", [B, L, E], F32)
    x1_s = dscr("x1_s", [B, L, E], F32)
    yrt_s = dscr("yrt_s", [2, CH, E], BF16)
    qrt_s = dscr("qrt_s", [2, CH, E], BF16)
    krt_s = dscr("krt_s", [2, CH, E], BF16)
    art_s = dscr("art_s", [2, CH, E], BF16)
    hid_s = dscr("hid_s", [L, FA], BF16)
    zq_s = dscr("zq_s", [L, E], BF16)
    zq2_s = dscr("zq2_s", [L, E], BF16)
    hq_s = dscr("hq_s", [L, FI], BF16)
    hq2_s = dscr("hq2_s", [L, FI], BF16)
    zq_b = [zq_s, zq2_s]
    hq_b = [hq_s, hq2_s]
    wqgT_s = dscr("wqgT_s", [E, FI], BF16)
    wquT_s = dscr("wquT_s", [E, FI], BF16)
    wqdT_s = dscr("wqdT_s", [FI, E], BF16)
    cc_in = nc.dram_tensor("cc_in", [1, 8], F32, kind="Internal")
    cc_out = nc.dram_tensor("cc_out", [1, 8], F32, kind="Internal")

    ctx = ExitStack()
    Ea = ctx.enter_context

    ARENA_COLS = 100992
    arena = Ea(nc.sbuf_tensor("arena", [128, ARENA_COLS], BF16))

    class Alloc:
        def __init__(self):
            self.off = 0
            self.marks = {}

        def bf(self, cols):
            a = self.off
            assert a + cols <= ARENA_COLS, f"arena overflow {a + cols}"
            self.off += (cols + 1) & ~1
            return arena[:, a:a + cols]

        def f32(self, elems):
            return self.bf(2 * elems).bitcast(F32)

    AL = Alloc()
    tab_ce = AL.f32(16 * 32)
    tab_co = AL.f32(16 * 32)
    tab_se = AL.f32(16 * 32)
    tab_so = AL.f32(16 * 32)
    P2_BASE = AL.off

    g1c = Ea(nc.sbuf_tensor("g1c", [128, NE], F32))
    g2c = Ea(nc.sbuf_tensor("g2c", [128, NE], F32))
    ghc = Ea(nc.sbuf_tensor("ghc", [128, NFA], F32))
    mgc = Ea(nc.sbuf_tensor("mgc", [128, NFA], F32))
    mgcb = Ea(nc.sbuf_tensor("mgcb", [128, NFA], BF16))
    iotac = Ea(nc.sbuf_tensor("iotac_sb", [128, NFA], F32))
    trim = Ea(nc.sbuf_tensor("trim_sb", [128, 128], BF16))
    idb = Ea(nc.sbuf_tensor("idb", [128, 128], BF16))
    wdp2r = Ea(nc.sbuf_tensor("wdp2r", [1, 256], F32))
    onesb = Ea(nc.sbuf_tensor("onesb", [128, 8], BF16))
    ones1b = Ea(nc.sbuf_tensor("ones1b", [1, 128], BF16))
    ones1f = Ea(nc.sbuf_tensor("ones1f", [1, 128], F32))
    stat = Ea(nc.sbuf_tensor("stat", [128, 160], F32))
    ssc = Ea(nc.sbuf_tensor("ssc", [128, 34], F32))
    sdp = Ea(nc.sbuf_tensor("sdp", [1, 1104], F32))
    sdpb = Ea(nc.sbuf_tensor("sdpb", [128, 16], BF16))
    s_sb = Ea(nc.sbuf_tensor("s_sb", [128, 528], BF16))
    qs = Ea(nc.sbuf_tensor("qs", [128, 192], F32))
    qsb = Ea(nc.sbuf_tensor("qsb", [128, 128], BF16))

    PS = [Ea(nc.psum_tensor(f"PS{i}", [128, 512], F32)) for i in range(6)]
    PSN = Ea(nc.psum_tensor("PSN", [128, 1024], F32))

    def sem(name):
        return Semw(Ea(nc.semaphore(name)), name)

    s_v = sem("s_v")
    s_sc = sem("s_sc")
    s_pe = sem("s_pe")
    s_cc = sem("s_cc")
    sCON = sem("sCON")
    sX0 = sem("sX0")
    sYW = [sem("sYW0"), sem("sYW1")]
    sQW = [sem("sQW0"), sem("sQW1")]
    sAW = [sem("sAW0"), sem("sAW1")]
    sXH = [sem("sXH0"), sem("sXH1")]
    sXW = sem("sXW")
    sRR = [sem("sRR0"), sem("sRR1")]
    sOut = sem("sOut")
    sWQ = [sem("sWQ0"), sem("sWQ1")]
    sWN = sem("sWN")
    sGW = sem("sGW")
    sGC = [sem("sGC0"), sem("sGC1")]
    sGA = [sem("sGA0"), sem("sGA1")]
    sGB = [sem("sGB0"), sem("sGB1")]
    sGD = sem("sGD")
    sGX = sem("sGX")
    sTa = sem("sTa")
    sTb = [sem("sTb0"), sem("sTb1")]
    Sched.DMA_SEMS = (sCON, sX0, sYW[0], sYW[1], sQW[0], sQW[1], sAW[0], sAW[1],
                      sXH[0], sXH[1], sXW, sRR[0], sRR[1], sOut, sWQ[0], sWQ[1],
                      sWN, sGW, sGC[0], sGC[1], sGA[0], sGA[1], sGB[0], sGB[1],
                      sGD, sGX, sTa, sTb[0], sTb[1])

    def mm(ps_ap, lhsT, rhs, start, stop, waits=(), inc=None, tp=None,
           skip_group_check=False):
        return S.op("tensor",
                    lambda e, o=ps_ap, l=lhsT, r=rhs, st=start, sp=stop, t=tp,
                    sg=skip_group_check:
                    e.matmul(o, l, r, start=st, stop=sp, tile_position=t,
                             skip_group_check=sg),
                    waits=waits, inc=inc)

    def vop(meth, *args, waits=(), inc=None, eng="vector", **kw):
        def fn(e, m=meth, a=args, k=kw):
            return getattr(e, m)(*a, **k)
        return S.op(eng, fn, waits=waits, inc=inc)

    def vops(meth, *args, waits=(), **kw):
        w = list(waits)
        if s_v.n > 0:
            w.append((s_v, s_v.n))
        return vop(meth, *args, waits=w, inc=s_v, **kw)

    def wsmall():
        return [(s_v, s_v.n)] if s_v.n > 0 else []

    def tap(nm, src_ap, waits):
        if nm in tap_d:
            S.dma("sync", tap_d[nm][:, :] if tap_d[nm].ndim == 2 else tap_d[nm],
                  src_ap, sOut, waits=waits)

    def newton_rsqrt(out_ap, v_ap, eps, w):
        t0 = stat[:, 128:128 + w]
        t1 = stat[:, 144:144 + w]
        assert w <= 16
        vops("tensor_scalar", t0, v_ap, float(eps), None, ALU.add)
        t0i = t0.bitcast(I32)
        t1i = t1.bitcast(I32)
        vops("tensor_copy", out_ap, t0i)
        vops("tensor_scalar", out_ap, out_ap, -0.5, float(0x5F3759DF),
             ALU.mult, ALU.add)
        vops("tensor_copy", t1i, out_ap)
        vops("tensor_copy", out_ap, t1)
        c = None
        for it in range(2):
            vops("tensor_tensor", t1, out_ap, out_ap, ALU.mult)
            vops("tensor_tensor", t1, t1, t0, ALU.mult)
            vops("tensor_scalar", t1, t1, -0.5, 1.5, ALU.mult, ALU.add)
            c = vops("tensor_tensor", out_ap, out_ap, t1, ALU.mult)
        return c

    wdp1_sb = AL.bf(NE * 256)
    P2_BASE = AL.off
    wqkv_sb = AL.bf(NE * 3072)
    wout_sb = AL.bf(NE * 1024)
    x0c = AL.bf(NT * 1024)
    ypb = AL.bf(1024)
    ybk = AL.bf(2 * NT * 1024)
    yfm = AL.bf(2 * NE * 512)
    qktm = AL.bf(2 * 2048)
    qkr = AL.bf(NT * 1024)
    kkr = AL.bf(NT * 1024)
    rsc = AL.bf(1536)
    qfm = AL.bf(8 * 512)
    kfm = AL.bf(8 * 512)
    vaug = AL.bf(NT * 1040)
    atm = AL.bf(4 * 128)
    attn_tm = AL.bf(NT * 1024)
    evst2 = attn_tm[:, 0:2048].bitcast(F32)
    attnfm = AL.bf(NE * 512)
    evst = AL.f32(1024)
    h2c = AL.bf(2 * 1024)
    assert AL.off <= ARENA_COLS, AL.off

    tabv = [tab_ce, tab_co, tab_se, tab_so]
    for i in range(4):
        S.dma("sync", tabv[i].rearrange("p (lt i) -> p lt i", i=32),
              tab_in[i * L:(i + 1) * L, :].rearrange("(lt p) i -> p lt i", p=128),
              sCON)
    S.dma("sync", trim[:, :], trim_in[:, :], sCON)
    S.dma("sync", iotac[:, :], iota_in[:, :], sCON)
    S.dma("sync", idb[:, :], ident_in[:, :], sCON)
    S.dma("sync", g1c[:, :], g1_in.rearrange("o (et p) -> (o p) et", p=128), sCON,
          allow_slow_non_contiguous=True)
    S.dma("sync", g2c[:, :], g2_in.rearrange("o (et p) -> (o p) et", p=128), sCON,
          allow_slow_non_contiguous=True)
    S.dma("sync", ghc[:, :], gh_in.rearrange("o (ft p) -> (o p) ft", p=128), sCON,
          allow_slow_non_contiguous=True)
    S.dma("sync", wdp2r[:, :], wdp2_in[:, :], sCON)
    c_const = sCON.n

    c_wqkv = S.dma("gpsimd", wqkv_sb.rearrange("p (et f) -> p et f", f=3 * E),
                   wqkv_in.rearrange("(et p) f -> p et f", p=128), sGW)
    c_wout = S.dma("gpsimd", wout_sb.rearrange("p (et f) -> p et f", f=E),
                   wout_in.rearrange("(et p) f -> p et f", p=128), sGW)
    c_wdp1 = S.dma("gpsimd", wdp1_sb.rearrange("p (et f) -> p et f", f=256),
                   wdp1_in.rearrange("(et p) f -> p et f", p=128), sGW)

    vops("memset", onesb[:, :], 1.0)
    vops("memset", ones1b[:, :], 1.0)
    c_ones1 = vops("memset", ones1f[:, :], 1.0)
    c_ones = vop("memset",
                 vaug.rearrange("p (t h c) -> p (t h) c", c=65, h=16)[:, :, 64:65], 1.0,
                 inc=s_v)
    for et in range(NE):
        w = [(sGW, c_wqkv), (sCON, c_const)] if et == 0 else ()
        vop("tensor_scalar", wqkv_sb[:, et * 3072:(et + 1) * 3072],
            wqkv_sb[:, et * 3072:(et + 1) * 3072], g1c[:, et:et + 1], None, ALU.mult,
            waits=w)
    for et in range(NE):
        w = [(sGW, c_wdp1)] if et == 0 else ()
        c_wfold = vop("tensor_scalar", wdp1_sb[:, et * 256:(et + 1) * 256],
                      wdp1_sb[:, et * 256:(et + 1) * 256], g2c[:, et:et + 1], None,
                      ALU.mult, waits=w, inc=s_v)

    wqx = AL.bf(2 * 512)
    wqa = AL.bf(512)
    wqxT = qkr
    assert AL.off <= ARENA_COLS, AL.off

    def quant_ops(xin, W_, a_buf, c1_buf, t7_buf, an_buf, waits=(), sc_waits=()):
        nb = W_ // 16
        Mf = qs[:, 0:nb]
        sf = qs[:, 64:64 + nb]
        rsf = qs[:, 128:128 + nb]
        vops("tensor_reduce", Mf.rearrange("p (g o) -> p g o", o=1),
             xin.rearrange("p (g z) -> p g z", z=16), AXX, ALU.max,
             apply_absolute_value=True, waits=waits)
        vops("tensor_scalar", sf, Mf, 1.0 / 6.0, 1e-12, ALU.mult, ALU.max)
        vops("reciprocal", rsf, sf)
        an = an_buf[:, 0:W_]
        c_an = vop("tensor_tensor", an.rearrange("p (g z) -> p g z", z=16),
                   xin.rearrange("p (g z) -> p g z", z=16),
                   rsf.rearrange("p (g o) -> p g o", o=1).broadcast_to([128, nb, 16]),
                   ALU.mult, waits=wsmall(), inc=s_v)
        a_ = a_buf[:, 0:W_]
        c_abs = vop("activation", a_, an, AF.Abs, eng="scalar",
                    waits=[(s_v, c_an)] + list(sc_waits), inc=s_sc)
        c1 = c1_buf[:, 0:W_]
        t7 = t7_buf[:, 0:W_]
        vop("tensor_scalar", c1, a_, 0.25, None, ALU.is_gt, waits=[(s_sc, c_abs)])
        vop("scalar_tensor_tensor", c1, a_, 0.75, c1, ALU.is_gt, ALU.add)
        vop("scalar_tensor_tensor", c1, a_, 1.25, c1, ALU.is_gt, ALU.add)
        vop("scalar_tensor_tensor", c1, a_, 1.75, c1, ALU.is_gt, ALU.add)
        vop("tensor_scalar", t7, a_, 5.0, 2.0, ALU.is_gt, ALU.mult)
        vop("scalar_tensor_tensor", t7, a_, 3.5, t7, ALU.is_gt, ALU.add)
        vop("scalar_tensor_tensor", t7, a_, 2.5, t7, ALU.is_gt, ALU.add)
        vop("scalar_tensor_tensor", c1, c1, 0.5, t7, ALU.mult, ALU.add)
        vop("tensor_tensor", c1.rearrange("p (g z) -> p g z", z=16),
            c1.rearrange("p (g z) -> p g z", z=16),
            sf.rearrange("p (g o) -> p g o", o=1).broadcast_to([128, nb, 16]),
            ALU.mult)
        vop("tensor_scalar", a_.bitcast(U16), an.bitcast(U16), 0x8000, None,
            ALU.bitwise_and)
        return vop("tensor_tensor", xin.bitcast(U16), c1.bitcast(U16),
                   a_.bitcast(U16), ALU.bitwise_or, inc=s_v)

    wq_jobs = []
    for ti in range(NFI):
        for ks in range(2):
            wq_jobs.append((wqg_in, ti, ks * 512, 512, wqgT_s))
    for ti in range(NFI):
        for ks in range(2):
            wq_jobs.append((wqu_in, ti, ks * 512, 512, wquT_s))
    for ti in range(NE):
        for ks in range(6):
            w_ = 512 if ks < 5 else 256
            wq_jobs.append((wqd_in, ti, ks * 512, w_, wqdT_s))
    wq_state = {"i": 0, "free": [None, None], "tfree": [None, None],
                "pend": None, "ps3": None, "t2free": None,
                "p2first_t": True, "p2first_q": True}

    def emit_wq_jobs(n, a_buf, c1_buf, t7_buf, an_buf):
        for _ in range(n):
            i = wq_state["i"]
            if i >= len(wq_jobs):
                return
            wq_state["i"] = i + 1
            src, ti, ko, w_, dstT = wq_jobs[i]
            sl2 = i % 2
            xin = wqx[:, sl2 * 512: sl2 * 512 + w_]
            cL = S.dma("gpsimd", xin, src[ti * 128:(ti + 1) * 128, ko:ko + w_],
                       sGC[sl2], waits=W(wq_state["free"][sl2]))
            cq = quant_ops(xin, w_, a_buf, c1_buf, t7_buf, an_buf,
                           waits=[(sGC[sl2], cL)], sc_waits=())
            nsub = w_ // 128
            ps = PS[sl2]
            pkey = "psA" if sl2 == 0 else "psB"
            c_T = None
            for t in range(nsub):
                w = []
                if t == 0:
                    w = [(s_v, cq)] + W(ST[pkey])
                    if i == 0:
                        w.append((sCON, c_const))
                c_T = mm(ps[:, t * 128:(t + 1) * 128],
                         xin[:, t * 128:(t + 1) * 128], idb[:, :],
                         start=True, stop=True, waits=w,
                         inc=(s_pe if t == nsub - 1 else None))
            wq_state["free"][sl2] = (s_pe, c_T)
            xt = wqxT[:, sl2 * 512: sl2 * 512 + w_]
            wq_ = [(sQW[0], 0), (sQW[1], 0)] if wq_state["tfree"][sl2] is None \
                else []
            c_evT = vop("copy", xt, ps[:, 0:w_], eng="scalar",
                        waits=[(s_pe, c_T)] + W(wq_state["tfree"][sl2]) + wq_,
                        inc=s_sc)
            ST[pkey] = (s_sc, c_evT)
            cw = S.dma("sync",
                       dstT[ko:ko + w_, ti * 128:(ti + 1) * 128]
                       .rearrange("(t p) n -> p t n", p=128),
                       xt.rearrange("p (t n) -> p t n", n=128),
                       sWN, waits=[(s_sc, c_evT)])
            wq_state["tfree"][sl2] = (sWN, cw)

    ST = {"x0c_free": None, "psA": None, "psB": None, "yrt": [None, None],
          "qrt": [None, None], "krt": [None, None], "art": [None, None],
          "ypb": [None, None], "qktm": [None, None], "h2c": [None, None],
          "atm": [None] * 4, "atb": [None, None], "psn": None,
          "evst": [None, None],
          "yfm": [None, None], "ybk": [None, None], "qfm": None, "kfm": None,
          "s_sb": None, "vaug": None, "attnfm": None, "rope_free": None,
          "rope_free2": None, "x0_ld": None, "yfm_rd": [None, None]}

    def W(*keys):
        r = []
        for k in keys:
            v = ST[k] if isinstance(k, str) else k
            if v is not None:
                r.append(v)
        return r

    c_h2_w = {}
    c_x1a_w = {}
    NCHT = B * NCH

    def emit_x0_load(i):
        bb, chh = divmod(i, NCH)
        c = S.dma("gpsimd", x0c.rearrange("p (t e) -> p t e", e=E),
                  x_in[bb, chh * CH:(chh + 1) * CH, :]
                  .rearrange("(t p) e -> p t e", p=128),
                  sX0, waits=W("x0c_free"))
        ST["x0_ld"] = (sX0, c)

    def emit_ln(i):
        bb, chh = divmod(i, NCH)
        sl = i % 2
        for tt in range(NT):
            xs = x0c[:, tt * 1024:(tt + 1) * 1024]
            w0 = W("x0_ld") if tt == 0 else []
            vops("bn_stats", stat[:, tt * 12: tt * 12 + 6], xs[:, 0:512],
                 waits=w0)
            vops("bn_stats", stat[:, tt * 12 + 6: tt * 12 + 12], xs[:, 512:1024])
        g4 = stat[:, 0:48].rearrange("p (t g c) -> p t g c", g=4, c=3)
        means = g4[:, :, :, 1]
        m2s = g4[:, :, :, 2]
        mt = stat[:, 64:68]
        vt = stat[:, 68:72]
        vops("reduce_sum", mt.rearrange("p (t o) -> p t o", o=1), means, AXX)
        vops("tensor_scalar", mt, mt, 0.25, None, ALU.mult)
        vops("reduce_sum", vt.rearrange("p (t o) -> p t o", o=1), m2s, AXX)
        vops("tensor_tensor", stat[:, 72:76], mt, mt, ALU.mult)
        t16 = stat[:, 80:96].rearrange("p (t g) -> p t g", g=4)
        vops("tensor_tensor", t16, means, means, ALU.mult)
        vops("reduce_sum", stat[:, 76:80].rearrange("p (t o) -> p t o", o=1),
             t16, AXX)
        vops("tensor_scalar", stat[:, 76:80], stat[:, 76:80], 256.0, None,
             ALU.mult)
        vops("tensor_tensor", vt, vt, stat[:, 76:80], ALU.add)
        vops("tensor_scalar", vt, vt, 1.0 / 1024.0, None, ALU.mult)
        vops("tensor_tensor", vt, vt, stat[:, 72:76], ALU.subtract)
        a1 = stat[:, 96:100]
        newton_rsqrt(a1, vt, 1e-5, 4)
        vops("tensor_tensor", stat[:, 100:104], a1, a1, ALU.mult)
        vops("tensor_tensor", stat[:, 100:104], stat[:, 100:104], vt, ALU.mult)
        a2 = stat[:, 104:108]
        newton_rsqrt(a2, stat[:, 100:104], 1e-6, 4)
        vops("tensor_scalar", a2, a2, 1e-6, 1e6, ALU.max, ALU.min)
        for tt in range(NT):
            xs = x0c[:, tt * 1024:(tt + 1) * 1024]
            yb = ybk[:, sl * 4096 + tt * 1024: sl * 4096 + (tt + 1) * 1024]
            c_y = vop("tensor_scalar", yb, xs, mt[:, tt:tt + 1],
                      a1[:, tt:tt + 1], ALU.subtract, ALU.mult,
                      waits=wsmall() + (W(ST["ybk"][sl]) if tt == 0 else []),
                      inc=s_v)
            yp = ypb[:, 0:1024]
            c_yp = vop("tensor_scalar", yp, yb, a2[:, tt:tt + 1], None, ALU.mult,
                       waits=W(ST["ypb"][0]), inc=s_v)
            cw = S.dma("sync", yrt_s[sl, tt * 128:(tt + 1) * 128, :], yp, sYW[sl],
                       waits=[(s_v, c_yp)] + (W(ST["yrt"][sl]) if tt == 0 else []))
            ST["ypb"][0] = (sYW[sl], cw)
            if i == 0:
                tap("y" + str(tt), yb, [(s_v, c_y)])
        ST["x0c_free"] = (s_v, s_v.n)
        c_ypb_w = sYW[sl].n
        c_yfm = None
        for et in range(NE):
            w = ([(sYW[sl], c_ypb_w)] if et == 0 else []) + \
                (W(ST["yfm"][sl]) if et == 0 else [])
            c_yfm = S.dmat("scalar", yfm[:, sl * 4096 + et * 512: sl * 4096 + (et + 1) * 512],
                           yrt_s[sl][:, et * 128:(et + 1) * 128], sTa, waits=w)
        ST["yrt"][sl] = (sTa, c_yfm)
        ST["yfm_rd"][sl] = (sTa, c_yfm)

    c_msS = None
    emit_x0_load(0)
    emit_ln(0)
    for i in range(NCHT):
        bb, chh = divmod(i, NCH)
        sl = i % 2
        lt0 = chh * NT
        first_of_b = (chh == 0)
        if i + 1 < NCHT:
            emit_x0_load(i + 1)
        if first_of_b:
            vops("memset", PS[4][:, :], 0.0, waits=W("s_sb"))
            c_msS = vops("memset", PS[5][:, :], 0.0)
        c_yfm = ST["yfm_rd"][sl]
        for tt in range(NT):
            qsl = tt % 2
            qk = qktm[:, qsl * 2048:(qsl + 1) * 2048]
            for fs in range(6):
                ps = PS[fs % 2]
                pfree = ST["psA"] if fs % 2 == 0 else ST["psB"]
                for et in range(NE):
                    w = []
                    if et == 0:
                        w = W(pfree)
                        if fs == 0 and tt == 0:
                            w += W(c_yfm)
                            if i == 0:
                                w.append((sGW, c_wqkv))
                                w.append((s_v, c_wfold))
                                w.append((s_v, c_ones1))
                    c_q = mm(ps[:, :],
                             yfm[:, sl * 4096 + et * 512 + tt * 128: sl * 4096 + et * 512 + tt * 128 + 128],
                             wqkv_sb[:, et * 3072 + fs * 512: et * 3072 + fs * 512 + 512],
                             start=(et == 0), stop=(et == NE - 1), waits=w,
                             inc=(s_pe if et == NE - 1 else None))
                if fs < 4:
                    wqk = W(ST["qktm"][qsl]) if fs == 0 else []
                    c_ev = vop("copy", qk[:, fs * 512:(fs + 1) * 512], ps[:, :],
                               eng="scalar", waits=[(s_pe, c_q)] + wqk, inc=s_sc)
                    if fs == 1:
                        c_evq = c_ev
                    if fs == 3:
                        c_evk = c_ev
                else:
                    dst = vaug[:, tt * 1040 + (fs - 4) * 520: tt * 1040 + (fs - 4) * 520 + 520] \
                        .rearrange("p (h c) -> p h c", c=65)[:, :, 0:64]
                    w = [(s_pe, c_q)]
                    if i == 0 and tt == 0 and fs == 4:
                        w.append((s_v, c_ones))
                    w += W(ST["vaug"]) if fs == 4 else []
                    c_ev = vop("copy", dst,
                               ps[:, :].rearrange("p (h c) -> p h c", c=64),
                               eng="scalar", waits=w, inc=s_sc)
                if fs % 2 == 0:
                    ST["psA"] = (s_sc, c_ev)
                else:
                    ST["psB"] = (s_sc, c_ev)
            c_qkev = s_sc.n
            lt = lt0 + tt
            for (src_off, dst) in ((0, qkr), (1024, kkr)):
                sv = qk[:, src_off:src_off + 1024].rearrange("p (h z) -> p h z", z=64)
                dv = dst[:, tt * 1024:(tt + 1) * 1024].rearrange("p (h z) -> p h z", z=64)
                Ein = sv[:, :, 0:32]
                Oin = sv[:, :, 32:64]
                Eo = dv[:, :, 0:32]
                Oo = dv[:, :, 32:64]
                t1 = rsc[:, 0:512].rearrange("p (h i) -> p h i", i=32)
                def tb(t):
                    return t[:, lt * 32:(lt + 1) * 32] \
                        .rearrange("p (o i) -> p o i", o=1).broadcast_to([128, 16, 32])
                wrope = [(s_sc, c_evq if src_off == 0 else c_evk)]
                if tt == 0 and src_off == 0:
                    wrope += W("rope_free", "rope_free2") + [(sWN, 0)]
                    if i == 0:
                        wrope.append((sCON, c_const))
                vop("tensor_tensor", t1, Oin, tb(tab_se), ALU.mult, waits=wrope)
                vop("tensor_tensor", Eo, Ein, tb(tab_ce), ALU.mult)
                vop("tensor_tensor", Eo, Eo, t1, ALU.subtract)
                vop("tensor_tensor", t1, Ein, tb(tab_so), ALU.mult)
                vop("tensor_tensor", Oo, Oin, tb(tab_co), ALU.mult)
                c_rope = vop("tensor_tensor", Oo, Oo, t1, ALU.add, inc=s_v)
                dfl = dst[:, tt * 1024:(tt + 1) * 1024]
                er = rsc[:, 512:1536]
                c_min = vop("tensor_scalar", er, dfl, 0.0, None, ALU.min, inc=s_v)
                c_elu_exp = vop("activation", er, er, AF.Exp, eng="scalar",
                                waits=[(s_v, c_min)], inc=s_sc)
                c_elu = vop("scalar_tensor_tensor", dfl, dfl, 0.0, er,
                            ALU.max, ALU.add, waits=[(s_sc, c_elu_exp)], inc=s_v)
            ST["qktm"][qsl] = (s_v, c_elu)
            cwq = S.dma("sync", qrt_s[sl, tt * 128:(tt + 1) * 128, :],
                        qkr[:, tt * 1024:(tt + 1) * 1024], sQW[sl],
                        waits=[(s_v, c_elu)] + W(ST["qrt"][sl] if tt == 0 else None))
            cwk = S.dma("sync", krt_s[sl, tt * 128:(tt + 1) * 128, :],
                        kkr[:, tt * 1024:(tt + 1) * 1024], sQW[sl],
                        waits=W(ST["krt"][sl] if tt == 0 else None))
            if i == 0 and tt == 0:
                tap("qkv0", qk[:, 0:2048], [(s_sc, c_qkev)])
                tap("qrope0", qkr[:, 0:1024], [(s_v, c_elu)])
        c_qk_w = sQW[sl].n
        ST["rope_free2"] = (sQW[sl], c_qk_w)
        for hp in range(8):
            w = [(sQW[sl], c_qk_w)] if hp == 0 else []
            c_qfm = S.dmat("scalar", qfm[:, hp * 512:(hp + 1) * 512],
                           qrt_s[sl][:, hp * 128:(hp + 1) * 128], sTa,
                           waits=w + (W("qfm") if hp == 0 else []))
        for hp in range(8):
            w = [(sQW[sl], c_qk_w)] if hp == 0 else []
            c_kfm = S.dmat("sync", kfm[:, hp * 512:(hp + 1) * 512],
                           krt_s[sl][:, hp * 128:(hp + 1) * 128], sTa,
                           waits=w + (W("kfm") if hp == 0 else []))
        ST["qfm"] = None
        ST["kfm"] = None
        qrt_reads = (sTa, c_kfm)
        ST["qrt"][sl] = qrt_reads
        ST["krt"][sl] = qrt_reads
        for c in range(NT):
            first = (chh == 0 and c == 0)
            for r in range(2):
                for hh in range(8):
                    h = r * 8 + hh
                    hp, hl = h // 2, h % 2
                    qsl_ = qfm[:, hp * 512 + c * 128: hp * 512 + c * 128 + 128]
                    ksl_ = kfm[:, hp * 512 + c * 128: hp * 512 + c * 128 + 128]
                    qh = qsl_[64 * hl:64 * hl + 64, :]
                    kh = ksl_[64 * hl:64 * hl + 64, :]
                    pa = PS[2 + (h % 2)]
                    w = W(ST["atb"][h % 2])
                    if h == 0 and c == 0:
                        w += [(sTa, c_qfm), (sTa, c_kfm)]
                    cA = mm(pa[:, 0:128], kh, qh, start=True, stop=True,
                            waits=w, inc=s_pe)
                    wm = [(s_pe, cA)] + W(ST["atm"][h % 4])
                    if i == 0 and c == 0 and h == 0:
                        wm.append((sCON, c_const))
                    cM = vop("tensor_tensor", atm[:, (h % 4) * 128:(h % 4) * 128 + 128],
                             pa[:, 0:128], trim[:, :], ALU.mult,
                             waits=wm, inc=s_v)
                    ST["atb"][h % 2] = (s_v, cM)
                    w = [(s_v, cM)]
                    if hh == 0:
                        w += W(ST["psn"])
                        if c == 0:
                            w.append((s_sc, c_qkev))
                    nreg = PSN[:, 128 * hh: 128 * hh + 65]
                    mm(nreg, atm[:, (h % 4) * 128:(h % 4) * 128 + 128],
                       vaug[:, c * 1040 + 65 * h: c * 1040 + 65 * h + 65],
                       start=True, stop=first, waits=w)
                    cI = s_pe.n
                    if not first:
                        cI = mm(nreg, qh,
                                s_sb[64 * hl:64 * hl + 64, 66 * (h // 2): 66 * (h // 2) + 65],
                                start=False, stop=True,
                                waits=W(ST["s_sb"]) if hh == 0 and r == 0 else (),
                                inc=s_pe)
                    ksl2 = kkr[:, c * 1024 + 64 * h: c * 1024 + 64 * h + 64]
                    wS = []
                    if h == 0:
                        wS = W(ST["s_sb"])
                        if first:
                            wS = wS + [(s_v, c_msS)]
                    mm(PS[4][64 * hl:64 * hl + 64, 64 * (h // 2): 64 * (h // 2) + 64],
                       ksl2, vaug[:, c * 1040 + 65 * h: c * 1040 + 65 * h + 64],
                       start=False, stop=True, tp=(0, 64 * hl),
                       skip_group_check=True, waits=wS)
                    c_supd = mm(PS[5][64 * hl:64 * hl + 64, (h // 2):(h // 2) + 1],
                                ksl2, onesb[:, 0:1],
                                start=False, stop=True,
                                skip_group_check=True,
                                tp=(0, 64 * hl), inc=s_pe)
                    ST["atm"][h % 4] = (s_pe, c_supd)
                den = PSN[:, 0:1024].rearrange("p (h s) -> p h s", s=128)[:, :, 64:65]
                vops("tensor_scalar",
                     ssc[:, 0:8].rearrange("p (h o) -> p h o", o=1),
                     den, 1e-6, None, ALU.max,
                     waits=[(s_pe, s_pe.n)])
                vops("reciprocal", ssc[:, 8:16], ssc[:, 0:8])
                dstv = attn_tm[:, c * 1024 + r * 512: c * 1024 + r * 512 + 512] \
                    .rearrange("p (h z) -> p h z", z=64)
                inv = PSN[:, 0:1024].rearrange("p (h s) -> p h s", s=128)[:, :, 0:64]
                rb = ssc[:, 8:16].rearrange("p (h o) -> p h o", o=1) \
                    .broadcast_to([128, 8, 64])
                wdv = wsmall()
                if c < 2 and r == 0:
                    wdv += [(sXH[0], 0), (sXH[1], 0)]
                cDiv = vop("tensor_tensor", dstv, inv, rb, ALU.mult,
                           waits=wdv, inc=s_v)
                ST["psn"] = (s_v, cDiv)
            w = [(s_pe, c_supd)]
            vop("copy",
                s_sb.rearrange("p (g c) -> p g c", c=66)[:, :, 0:64],
                PS[4][:, :].rearrange("p (g z) -> p g z", z=64),
                eng="scalar", waits=w)
            c_sevac = vop("copy",
                          s_sb.rearrange("p (g c) -> p g c", c=66)[:, :, 64:65],
                          PS[5][:, 0:8].rearrange("p (g o) -> p g o", o=1),
                          eng="scalar", inc=s_sc)
            ST["s_sb"] = (s_sc, c_sevac)
            c_aw = S.dma("sync", art_s[sl, c * 128:(c + 1) * 128, :],
                         attn_tm[:, c * 1024:(c + 1) * 1024], sAW[sl],
                         waits=[(s_v, cDiv)] + (W(ST["art"][sl]) if c == 0 else []))
            if c == 2 and i + 1 < NCHT:
                emit_ln(i + 1)
        c_attn_done = s_pe.n
        ST["qfm"] = (s_pe, c_attn_done)
        ST["kfm"] = (s_pe, c_attn_done)
        ST["vaug"] = (s_pe, c_attn_done)
        ST["rope_free"] = (s_pe, c_attn_done)
        ST["art"][sl] = None
        c_art_w = sAW[sl].n
        if i == 0:
            tap("attn0", attn_tm[:, 0:1024], [(s_v, cDiv)])
        for et in range(NE):
            w = [(sAW[sl], c_art_w)] if et == 0 else []
            c_afm = S.dmat("sync", attnfm[:, et * 512:(et + 1) * 512],
                           art_s[sl][:, et * 128:(et + 1) * 128], sTa,
                           waits=w + (W("attnfm") if et == 0 else []))
        ST["art"][sl] = (sTa, c_afm)
        ST["attnfm"] = None
        for tt in range(NT):
            ev = evst if tt % 2 == 0 else evst2
            for half in range(2):
                ps = PS[half]
                pfree = ST["psA"] if half == 0 else ST["psB"]
                for et in range(NE):
                    w = []
                    if et == 0:
                        w = W(pfree)
                        if tt == 0 and half == 0:
                            w.append((sTa, c_afm))
                            if i == 0:
                                w.append((sGW, c_wout))
                    c_wo = mm(ps[:, :],
                              attnfm[:, et * 512 + tt * 128: et * 512 + tt * 128 + 128],
                              wout_sb[:, et * 1024 + half * 512: et * 1024 + half * 512 + 512],
                              start=(et == 0), stop=(et == NE - 1), waits=w,
                              inc=(s_pe if et == NE - 1 else None))
                yslice = ybk[:, sl * 4096 + tt * 1024 + half * 512: sl * 4096 + tt * 1024 + half * 512 + 512]
                wev = [(s_pe, c_wo)] + W(ST["evst"][tt % 2])
                if tt == 1 and half == 0:
                    wev.append((sAW[sl], c_art_w))
                c_ev = vop("scalar_tensor_tensor",
                           ev[:, half * 512:(half + 1) * 512], ps[:, :], 1.0,
                           yslice, ALU.mult, ALU.add,
                           waits=wev, inc=s_v)
                if half == 0:
                    ST["psA"] = (s_v, c_ev)
                else:
                    ST["psB"] = (s_v, c_ev)
            c_x1aw = S.dma("sync", x1a_s[bb, chh * CH + tt * 128: chh * CH + (tt + 1) * 128, :],
                           ev[:, :], sXH[bb], waits=[(s_v, c_ev)])
            c_x1a_w[(bb, chh, tt)] = c_x1aw
            vops("bn_stats", stat[:, 32:38], ev[:, 0:512], waits=[(s_v, c_ev)])
            vops("bn_stats", stat[:, 38:44], ev[:, 512:1024])
            g3 = stat[:, 32:44].rearrange("p (g c) -> p g c", c=3)
            means = g3[:, :, 1]
            m2s = g3[:, :, 2]
            msq = stat[:, 56:57]
            vops("tensor_tensor", stat[:, 44:48], means, means, ALU.mult)
            vops("reduce_sum", msq, stat[:, 44:48], AXX)
            vops("tensor_scalar", msq, msq, 256.0, None, ALU.mult)
            vops("reduce_sum", stat[:, 57:58], m2s, AXX)
            vops("tensor_tensor", msq, msq, stat[:, 57:58], ALU.add)
            vops("tensor_scalar", msq, msq, 1.0 / 1024.0, None, ALU.mult)
            rinv2 = stat[:, 58:59]
            newton_rsqrt(rinv2, msq, 1e-6, 1)
            vops("tensor_scalar", rinv2, rinv2, 1e-6, 1e6, ALU.max, ALU.min)
            hs = h2c[:, (tt % 2) * 1024:(tt % 2) * 1024 + 1024]
            c_h2 = vop("tensor_scalar", hs, ev[:, :], rinv2, None, ALU.mult,
                       waits=W(ST["h2c"][tt % 2]) + wsmall(), inc=s_v)
            ST["evst"][tt % 2] = (s_v, c_h2)
            c_hw = S.dma("sync", h2_s[bb, chh * CH + tt * 128: chh * CH + (tt + 1) * 128, :],
                         hs, sXH[bb], waits=[(s_v, c_h2)])
            ST["h2c"][tt % 2] = (sXH[bb], c_hw)
            c_h2_w[(bb, chh, tt)] = c_hw
            if i == 0 and tt == 0:
                tap("x1a0", ev[:, :], [(s_v, c_ev)])
                tap("h20", hs, [(s_v, c_h2)])
        ST["ybk"][sl] = (s_v, s_v.n)
        ST["yfm"][sl] = (s_pe, c_attn_done)
        emit_wq_jobs(8, rsc[:, 0:512], rsc[:, 512:1024], rsc[:, 1024:1536], wqa)
    c_p1_done_v = s_v.n
    c_p1_done_pe = s_pe.n
    c_p1_done_sc = s_sc.n


    AL.off = P2_BASE
    h2fm = AL.bf(NE * 2048)
    h2fmB = AL.bf(NE * 2048)
    dacc = AL.bf(16 * 1024)
    silu_g = AL.bf(2 * 1024)
    hidg = AL.bf(9 * 1024)
    hsq = AL.bf(2 * 1024)
    wgsl = AL.bf(2 * 1024)
    wusl = AL.bf(2 * 1024)
    wdg = AL.bf(9 * 1024)
    x1ar = AL.f32(2 * 1024)
    x0r = AL.f32(2 * 1024)
    xst = AL.f32(2 * 1024)
    qsc2 = AL.bf(3 * 512)
    wqxT2 = AL.bf(512)
    assert AL.off <= ARENA_COLS, AL.off
    h2b_ = [h2fm, h2fmB]

    CBAR = [(s_pe, c_p1_done_pe), (s_v, c_p1_done_v), (s_sc, c_p1_done_sc)]
    BAR = (CBAR +
           [(q, q.n) for q in (sX0, sYW[0], sYW[1], sQW[0], sQW[1], sAW[0],
                               sAW[1], sXH[0], sXH[1], sTa, sWN, sGC[0],
                               sGC[1], sGW)])

    sig_done = None
    c_h2f_b = [None, None]
    for b in range(B):
        for et in range(NE):
            w = ([(sXH[b], c_h2_w[(b, NCH - 1, NT - 1)])] + CBAR) \
                if et == 0 else []
            c_h2f_b[b] = S.dmat("scalar", h2b_[b][:, et * 2048:(et + 1) * 2048],
                                h2_s[b][:, et * 128:(et + 1) * 128], sTb[b],
                                waits=w)
    for b in range(B):
        for et in range(NE):
            w = [(sTb[b], c_h2f_b[b])] if et == 0 else []
            vops("reduce_sum", ssc[:, 16 + et:17 + et],
                 h2b_[b][:, et * 2048:(et + 1) * 2048], AXX, waits=w)
        vops("tensor_scalar", ssc[:, 16:24], ssc[:, 16:24], 1.0 / 2048.0, None, ALU.mult)
        c_xm = vops("tensor_copy", sdpb[:, 8 * b:8 * b + 8], ssc[:, 16:24])
        dA = 520 * b
        for et in range(NE):
            c_dp1 = mm(PS[3][0:1, 0:256], sdpb[:, 8 * b + et:8 * b + et + 1],
                       wdp1_sb[:, et * 256:(et + 1) * 256],
                       start=(et == 0), stop=(et == NE - 1),
                       waits=([(s_v, c_xm)] + BAR) if et == 0 else (),
                       inc=(s_pe if et == NE - 1 else None))
        c_sil = vop("activation", sdp[0:1, dA:dA + 256], PS[3][0:1, 0:256], AF.Silu,
                    eng="scalar", waits=[(s_pe, c_dp1)] + (BAR if b == 0 else []),
                    inc=s_sc)
        vops("tensor_tensor", sdp[0:1, dA + 256:dA + 512], sdp[0:1, dA:dA + 256],
             wdp2r[0:1, :], ALU.mult, waits=[(s_sc, c_sil)])
        vops("reduce_sum", sdp[0:1, dA + 512:dA + 513], sdp[0:1, dA + 256:dA + 512], AXX)
        c_dp2 = vops("tensor_scalar", sdp[0:1, dA + 513:dA + 514],
                     sdp[0:1, dA + 512:dA + 513], 0.5, None, ALU.mult)
        c_th = vop("activation", sdp[0:1, dA + 514:dA + 515],
                   sdp[0:1, dA + 513:dA + 514], AF.Tanh, eng="scalar",
                   waits=[(s_v, c_dp2)], inc=s_sc)
        sig_done = vops("tensor_scalar", sdp[0:1, 1040 + b:1041 + b],
                        sdp[0:1, dA + 514:dA + 515],
                        0.5, 0.5, ALU.mult, ALU.add, waits=[(s_sc, c_th)])
    PF = {}
    for f_ in range(2):
        sl2 = f_ % 2
        w0 = list(CBAR) if f_ == 0 else []
        cwg_p = S.dma("gpsimd", wgsl[:, sl2 * 1024:(sl2 + 1) * 1024]
                      .rearrange("p (et k) -> p et k", k=128),
                      wga_in[:, f_ * 128:(f_ + 1) * 128]
                      .rearrange("(et p) k -> p et k", p=128),
                      sGA[sl2], waits=w0)
        cwu_p = S.dma("gpsimd", wusl[:, sl2 * 1024:(sl2 + 1) * 1024]
                      .rearrange("p (et k) -> p et k", k=128),
                      wua_in[:, f_ * 128:(f_ + 1) * 128]
                      .rearrange("(et p) k -> p et k", p=128), sGA[sl2])
        cwd_p = S.dma("gpsimd", wdg[:, f_ * 1024:(f_ + 1) * 1024],
                      wda_in[f_ * 128:(f_ + 1) * 128, :], sGA[sl2])
        PF[f_] = (cwg_p, cwu_p, cwd_p)

    vops("memset", sdp[0:1, 1056:1064], 0.0)
    c_st = vops("tensor_tensor", sdp[0:1, 1056:1057], sdp[0:1, 1040:1041],
                sdp[0:1, 1041:1042], ALU.add)
    c_cc_in = S.dma("gpsimd", cc_in[:, :], sdp[0:1, 1056:1064], sGX,
                    waits=[(s_v, c_st)])
    def _ccfn(e):
        e.collective_compute(
            "AllReduce", ALU.add, replica_groups=[list(range(8))],
            ins=[cc_in.ap().opt()], outs=[cc_out.ap().opt()]).then_inc(s_cc.h)
        return None
    S.op("gpsimd", _ccfn, waits=[(sGX, c_cc_in)])
    s_cc.n = 1
    c_cc_rd = S.dma("gpsimd", sdp[0:1, 1072:1080], cc_out[:, :], sGX,
                    waits=[(s_cc, 1)])
    vops("tensor_scalar", sdp[0:1, 1088:1089], sdp[0:1, 1072:1073], 192.0, 1536.0,
         ALU.mult, ALU.add, waits=[(sGX, c_cc_rd)])
    c_ym1 = vops("tensor_scalar", sdp[0:1, 1089:1090], sdp[0:1, 1088:1089], 1.0, None,
                 ALU.subtract)
    c_bc = mm(PS[3][0:128, 256:257], ones1f[0:1, :], sdp[0:1, 1089:1090],
              start=True, stop=True, waits=[(s_v, c_ym1)], inc=s_pe)
    c_bc2 = vops("tensor_copy", ssc[:, 24:25], PS[3][0:128, 256:257],
                 waits=[(s_pe, c_bc)])
    vops("tensor_single_scalar", mgc[:, :], iotac[:, :], ssc[:, 24:25], ALU.is_le)
    c_mg = vops("tensor_copy", mgcb[:, :], mgc[:, :])
    vops("reduce_sum", ssc[:, 30:31], mgc[:, :], AXX)
    c_ms = vops("tensor_copy", sdpb[:, 8:9], ssc[:, 30:31])
    c_sz = mm(PS[3][0:1, 300:301], sdpb[:, 8:9], onesb[:, 0:1], start=True,
              stop=True, waits=[(s_v, c_ms)], inc=s_pe)
    c_sz2 = vops("tensor_copy", sdp[0:1, 1090:1091], PS[3][0:1, 300:301],
                 waits=[(s_pe, c_sz)])
    c_rsz = vops("reciprocal", sdp[0:1, 1091:1092], sdp[0:1, 1090:1091])
    c_rb = mm(PS[3][0:128, 257:258], ones1f[0:1, :], sdp[0:1, 1091:1092],
              start=True, stop=True, waits=[(s_v, c_rsz)], inc=s_pe)
    c_rsbc = vops("tensor_copy", ssc[:, 25:26], PS[3][0:128, 257:258],
                  waits=[(s_pe, c_rb)])
    tap("sizes", sdp[0:1, 1040:1072], [(s_v, c_rsbc)])

    def emit_wq_job_p2():
        p = wq_state["pend"]
        if p is not None:
            wq_state["pend"] = None
            (xin, w_, dstT, ti, ko, cq, sl2) = p
            nsub = w_ // 128
            c_T = None
            for t in range(nsub):
                w = []
                if t == 0:
                    w = [(s_v, cq)] + W(wq_state["ps3"])
                    if wq_state["p2first_t"]:
                        w.append((s_v, c_rsbc))
                c_T = mm(PS[3][:, t * 128:(t + 1) * 128],
                         xin[:, t * 128:(t + 1) * 128], idb[:, :],
                         start=True, stop=True, waits=w,
                         inc=(s_pe if t == nsub - 1 else None))
            wq_state["free"][sl2] = (s_pe, c_T)
            wev = [(s_pe, c_T)] + W(wq_state["t2free"])
            if wq_state["p2first_t"]:
                wev += [(sXH[0], 0), (sXH[1], 0)]
                wq_state["p2first_t"] = False
            c_evT = vop("copy", wqxT2[:, 0:w_], PS[3][:, 0:w_], eng="scalar",
                        waits=wev, inc=s_sc)
            wq_state["ps3"] = (s_sc, c_evT)
            cw = S.dma("sync",
                       dstT[ko:ko + w_, ti * 128:(ti + 1) * 128]
                       .rearrange("(t p) n -> p t n", p=128),
                       wqxT2[:, 0:w_].rearrange("p (t n) -> p t n", n=128),
                       sWN, waits=[(s_sc, c_evT)])
            wq_state["t2free"] = (sWN, cw)
        i = wq_state["i"]
        if i >= len(wq_jobs):
            return
        wq_state["i"] = i + 1
        src, ti, ko, w_, dstT = wq_jobs[i]
        sl2 = i % 2
        xin = wqx[:, sl2 * 512: sl2 * 512 + w_]
        cL = S.dma("gpsimd", xin, src[ti * 128:(ti + 1) * 128, ko:ko + w_],
                   sGC[sl2], waits=W(wq_state["free"][sl2]))
        wq2 = []
        if wq_state["p2first_q"]:
            wq2 = [(s_pe, c_p1_done_pe), (sXH[0], 0), (sXH[1], 0)]
            wq_state["p2first_q"] = False
        cq = quant_ops(xin, w_, qsc2[:, 0:512], qsc2[:, 512:1024],
                       qsc2[:, 1024:1536], wqa,
                       waits=[(sGC[sl2], cL)] + wq2)
        wq_state["pend"] = (xin, w_, dstT, ti, ko, cq, sl2)

    zqfm = h2fm
    zx2 = arena[:, 0:4096].bitcast(F32)
    zzst = arena[:, 4096:5120]
    za_e = arena[:, 5120:6144]
    zc1_e = arena[:, 96256:97280]
    zt7_e = arena[:, 97280:98304]
    zan_e = arena[:, 98304:99328]
    ST5 = {"x1r": [None, None], "zst": None, "first": True, "rt_first": True}
    c_zqf_t_b = [{}, {}]
    c_x1_w = {}

    def emit_ln2_pair_early(b_, t0):
        first = ST5["first"]
        ST5["first"] = False
        x1vs = []
        for k in range(2):
            tt = t0 + k
            rs_ = tt % 2
            x1v = zx2[:, rs_ * 1024:(rs_ + 1) * 1024]
            x1vs.append(x1v)
            wx = [(sXW, c_x1_w[(b_, tt // 8, tt % 8)])] + W(ST5["x1r"][rs_])
            if first:
                wx += [(s_v, c_p1_done_v), (s_pe, c_rb)]
            c_xr = S.dma("sync", x1v, x1_s[b_, tt * 128:(tt + 1) * 128, :],
                         sRR[rs_], waits=wx)
            w0 = [(sRR[rs_], c_xr)]
            vops("bn_stats", stat[:, k * 12: k * 12 + 6], x1v[:, 0:512],
                 waits=w0)
            vops("bn_stats", stat[:, k * 12 + 6: k * 12 + 12], x1v[:, 512:1024])
        yield
        g2v = stat[:, 0:24].rearrange("p (t g c) -> p t g c", g=4, c=3)
        means = g2v[:, :, :, 1]
        m2s = g2v[:, :, :, 2]
        mt = stat[:, 32:34]
        vt = stat[:, 34:36]
        vops("reduce_sum", mt.rearrange("p (t o) -> p t o", o=1), means, AXX)
        vops("tensor_scalar", mt, mt, 0.25, None, ALU.mult)
        vops("reduce_sum", vt.rearrange("p (t o) -> p t o", o=1), m2s, AXX)
        vops("tensor_tensor", stat[:, 36:38], mt, mt, ALU.mult)
        t8 = stat[:, 112:120].rearrange("p (t g) -> p t g", g=4)
        vops("tensor_tensor", t8, means, means, ALU.mult)
        vops("reduce_sum", stat[:, 38:40].rearrange("p (t o) -> p t o", o=1),
             t8, AXX)
        vops("tensor_scalar", stat[:, 38:40], stat[:, 38:40], 256.0, None,
             ALU.mult)
        vops("tensor_tensor", vt, vt, stat[:, 38:40], ALU.add)
        vops("tensor_scalar", vt, vt, 1.0 / 1024.0, None, ALU.mult)
        vops("tensor_tensor", vt, vt, stat[:, 36:38], ALU.subtract)
        a1 = stat[:, 156:158]
        newton_rsqrt(a1, vt, 1e-5, 2)
        yield
        for k in range(2):
            if k == 1:
                yield
            tt = t0 + k
            rs_ = tt % 2
            c_z = vop("tensor_scalar", zzst[:, :], x1vs[k], mt[:, k:k + 1],
                      a1[:, k:k + 1], ALU.subtract, ALU.mult,
                      waits=W(ST5["zst"]) + wsmall(), inc=s_v)
            ST5["x1r"][rs_] = (s_v, c_z)
            wq_ = []
            if first and k == 0:
                wq_ = [(sXH[0], 0), (sXH[1], 0)]
            cq = quant_ops(zzst[:, :], 1024, za_e, zc1_e, zt7_e, zan_e,
                           waits=wq_,
                           sc_waits=[(s_pe, c_rb)] if first and k == 0 else ())
            cw = S.dma("sync", zq_b[b_][tt * 128:(tt + 1) * 128, :], zzst[:, :],
                       sWQ[b_], waits=[(s_v, cq)])
            ST5["zst"] = (sWQ[b_], cw)
            if b_ == 0 and tt == 0:
                tap("zq0", zzst[:, :], [(s_v, cq)])
            if b_ == 0 and tt % 4 == 3:
                t0_ = tt - 3
                c_zqf = None
                for et in range(NE):
                    w = [(sWQ[0], cw)] if et == 0 else []
                    if et == 0 and ST5["rt_first"]:
                        w.append((s_pe, c_p2a_b0_pe[0]))
                        ST5["rt_first"] = False
                    c_zqf = S.dmat("scalar",
                                   zqfm[:, et * 2048 + t0_ * 128:
                                        et * 2048 + (tt + 1) * 128],
                                   zq_b[0][t0_ * 128:(tt + 1) * 128,
                                           et * 128:(et + 1) * 128],
                                   sTb[0], waits=w)
                c_zqf_t_b[0][tt // 4] = c_zqf

    ST3 = {"wg": [None, None], "wdg": None,
           "sg": [None, None], "hsq": [None, None], "hidg": None,
           "pg": None, "pu": None, "pd": [None, None], "x1r": [None, None],
           "x0r": [None, None], "xst": [None, None], "ps2": None}
    c_p2a_b0_pe = [None]
    PD0 = {"gen": None, "n": 0}

    def pair_drv(PD, b_, start_now, limit):
        g_ = PD["gen"]
        if g_ is not None:
            if next(g_, "end") == "end":
                PD["gen"] = None
            return
        if start_now and PD["n"] < limit:
            PD["gen"] = emit_ln2_pair_early(b_, 2 * PD["n"])
            PD["n"] += 1
            next(PD["gen"])
    for b in range(B):
        h2f = h2b_[b]
        for et in range(NE):
            w = [(sTb[b], c_h2f_b[b])] if et == 0 else []
            c_g2 = vop("tensor_scalar", h2f[:, et * 2048:(et + 1) * 2048],
                       h2f[:, et * 2048:(et + 1) * 2048], g2c[:, et:et + 1], None,
                       ALU.mult, waits=w, inc=s_v)
        c_msQ = vops("memset", PS[2][:, :], 0.0, waits=W(ST3["ps2"]))
        for th in range(2):
            for fg in range(4):
                for j in range(9):
                    f = fg * 9 + j
                    sl2 = f % 2
                    if b == 0 and th == 0 and fg == 0 and j < 2:
                        cwg, cwu, cwd = PF[f]
                    else:
                        cwg = S.dma("gpsimd", wgsl[:, sl2 * 1024:(sl2 + 1) * 1024]
                                    .rearrange("p (et k) -> p et k", k=128),
                                    wga_in[:, f * 128:(f + 1) * 128]
                                    .rearrange("(et p) k -> p et k", p=128),
                                    sGA[sl2], waits=W(ST3["wg"][sl2]))
                        cwu = S.dma("gpsimd", wusl[:, sl2 * 1024:(sl2 + 1) * 1024]
                                    .rearrange("p (et k) -> p et k", k=128),
                                    wua_in[:, f * 128:(f + 1) * 128]
                                    .rearrange("(et p) k -> p et k", p=128), sGA[sl2])
                        cwd = S.dma("gpsimd", wdg[:, j * 1024:(j + 1) * 1024],
                                    wda_in[f * 128:(f + 1) * 128, :], sGA[sl2],
                                    waits=W(ST3["wdg"]) if j == 0 else ())
                    c_gh = vop("tensor_scalar", wdg[:, j * 1024:(j + 1) * 1024],
                               wdg[:, j * 1024:(j + 1) * 1024],
                               ghc[:, f:f + 1], None, ALU.mult,
                               waits=[(sGA[sl2], cwd)] +
                               ([(sCON, c_const)] if b == 0 and th == 0 and f == 0 else []),
                               inc=s_v)
                    for ts in range(2):
                        for et in range(NE):
                            w = []
                            if et == 0 and ts == 0:
                                w = [(sGA[sl2], cwg)] + W(ST3["pg"])
                                if fg == 0 and j == 0 and th == 0:
                                    w.append((s_v, c_g2))
                                    if b == 0:
                                        w.append((s_v, c_p1_done_v))
                            c_gmm = mm(PSN[:, ts * 512:(ts + 1) * 512],
                                       wgsl[:, sl2 * 1024 + et * 128: sl2 * 1024 + et * 128 + 128],
                                       h2f[:, et * 2048 + th * 1024 + ts * 512:
                                           et * 2048 + th * 1024 + (ts + 1) * 512],
                                       start=(et == 0), stop=(et == NE - 1), waits=w,
                                       inc=(s_pe if et == NE - 1 else None))
                    for ts in range(2):
                        for et in range(NE):
                            w = []
                            if et == 0 and ts == 0:
                                w = [(sGA[sl2], cwu)] + W(ST3["pu"])
                                if b == 0 and th == 0 and f == 0:
                                    w.append((s_sc, c_p1_done_sc))
                            c_umm = mm(PS[ts][:, :],
                                       wusl[:, sl2 * 1024 + et * 128: sl2 * 1024 + et * 128 + 128],
                                       h2f[:, et * 2048 + th * 1024 + ts * 512:
                                           et * 2048 + th * 1024 + (ts + 1) * 512],
                                       start=(et == 0), stop=(et == NE - 1), waits=w,
                                       inc=(s_pe if et == NE - 1 else None))
                    c_si = None
                    for ts in range(2):
                        c_si = vop("activation", silu_g[:, sl2 * 1024 + ts * 512:
                                                        sl2 * 1024 + (ts + 1) * 512],
                                   PSN[:, ts * 512:(ts + 1) * 512], AF.Silu,
                                   eng="scalar",
                                   waits=([(s_pe, c_gmm)] + W(ST3["sg"][sl2]))
                                   if ts == 0 else (), inc=s_sc)
                    c_hid = None
                    for ts in range(2):
                        c_hid = vop("scalar_tensor_tensor",
                                    hidg[:, j * 1024 + ts * 512: j * 1024 + (ts + 1) * 512],
                                    silu_g[:, sl2 * 1024 + ts * 512: sl2 * 1024 + (ts + 1) * 512],
                                    mgc[:, f:f + 1], PS[ts][:, :], ALU.mult, ALU.mult,
                                    waits=[(s_sc, c_si), (s_pe, c_umm)] +
                                    ([(s_v, c_mg)] if b == 0 and th == 0 and f == 0 else []) +
                                    ([(sWN, 0)] if b == 0 and th == 0 and fg == 0
                                     and j == 7 and ts == 0 else []) +
                                    (W(ST3["hidg"]) if j == 0 and ts == 0 else []),
                                    inc=s_v)
                    ST3["sg"][sl2] = (s_v, c_hid)
                    ST3["pg"] = (s_sc, c_si)
                    ST3["pu"] = (s_v, c_hid)
                    c_q2 = None
                    for ts in range(2):
                        c_q2 = vop("activation", hsq[:, sl2 * 1024 + ts * 512:
                                                     sl2 * 1024 + (ts + 1) * 512],
                                   hidg[:, j * 1024 + ts * 512: j * 1024 + (ts + 1) * 512],
                                   AF.Square, eng="scalar",
                                   waits=([(s_v, c_hid)] + W(ST3["hsq"][sl2]))
                                   if ts == 0 else (), inc=s_sc)
                    for tk in range(8):
                        wq_ = [(s_sc, c_q2)] if tk == 0 else []
                        if f == 0 and tk == 0 and th == 0:
                            wq_ += [(s_v, c_msQ)]
                        c_ssq = mm(PS[2][:, th * 8 + tk: th * 8 + tk + 1],
                                   hsq[:, sl2 * 1024 + tk * 128: sl2 * 1024 + tk * 128 + 128],
                                   onesb[:, 0:1], start=False, stop=True,
                                   skip_group_check=True, waits=wq_,
                                   inc=(s_pe if tk == 7 else None))
                    ST3["hsq"][sl2] = (s_pe, c_ssq)
                    ST3["wg"][sl2] = (s_pe, c_umm)
                    if b == 0:
                        emit_wq_job_p2()
                    else:
                        inst1 = (th * 4 + fg) * 9 + j
                        if inst1 % 2 == 0 or PD0["gen"] is None:
                            pair_drv(PD0, 0, inst1 % 9 == 4, 8)
                if fg == 3:
                    vops("tensor_scalar", stat[:, 40:48],
                         PS[2][:, th * 8: th * 8 + 8], ssc[:, 25:26], None,
                         ALU.mult, waits=[(s_pe, c_ssq)])
                    newton_rsqrt(stat[:, 48:56], stat[:, 40:48], 1e-6, 8)
                    c_ri = vops("tensor_scalar", stat[:, 48:56], stat[:, 48:56],
                                1e-6, 1e6, ALU.max, ALU.min)
                    ST3["ps2"] = (s_v, c_ri)
                c_dn = None
                for tk in range(8):
                    if fg == 3:
                        rs_ = tk % 2
                        rows0 = b, th * 1024 + tk * 128
                        c_x1r = S.dma("sync", x1ar[:, rs_ * 1024:(rs_ + 1) * 1024],
                                      x1a_s[b, th * 1024 + tk * 128: th * 1024 + (tk + 1) * 128, :],
                                      sRR[rs_], waits=W(ST3["x1r"][rs_]) +
                                      [(sXH[b], sXH[b].n)])
                        c_x0r = S.dma("sync", x0r[:, rs_ * 1024:(rs_ + 1) * 1024],
                                      x_in[b, th * 1024 + tk * 128: th * 1024 + (tk + 1) * 128, :],
                                      sRR[rs_], waits=W(ST3["x0r"][rs_]) +
                                      ([(sAW[0], 0), (sAW[1], 0)]
                                       if b == 0 and th == 0 and tk < 2 else []))
                    for j in range(9):
                        for eh in range(2):
                            w = []
                            if j == 0 and eh == 0:
                                w = W(ST3["pd"][0], ST3["pd"][1])
                                if tk == 0:
                                    w += [(s_v, c_gh)]
                            c_dn = mm(PS[4 + eh][:, :],
                                      hidg[:, j * 1024 + tk * 128: j * 1024 + tk * 128 + 128],
                                      wdg[:, j * 1024 + eh * 512: j * 1024 + (eh + 1) * 512],
                                      start=(j == 0), stop=(j == 8), waits=w,
                                      inc=(s_pe if (j == 8) else None))
                    if fg < 3:
                        for eh in range(2):
                            if fg == 0:
                                c_ev = vop("tensor_copy",
                                           dacc[:, (th * 8 + tk) * 1024 + eh * 512:
                                                (th * 8 + tk) * 1024 + (eh + 1) * 512],
                                           PS[4 + eh][:, :],
                                           waits=[(s_pe, c_dn)] if eh == 0 else (),
                                           inc=s_v)
                            else:
                                c_ev = vop("scalar_tensor_tensor",
                                           dacc[:, (th * 8 + tk) * 1024 + eh * 512:
                                                (th * 8 + tk) * 1024 + (eh + 1) * 512],
                                           PS[4 + eh][:, :], 1.0,
                                           dacc[:, (th * 8 + tk) * 1024 + eh * 512:
                                                (th * 8 + tk) * 1024 + (eh + 1) * 512],
                                           ALU.mult, ALU.add,
                                           waits=[(s_pe, c_dn)] if eh == 0 else (),
                                           inc=s_v)
                            ST3["pd"][eh] = (s_v, c_ev)
                    else:
                        rs_ = tk % 2
                        xs_ = xst[:, rs_ * 1024:(rs_ + 1) * 1024]
                        for eh in range(2):
                            w = [(s_pe, c_dn), (s_v, c_ri), (sRR[rs_], c_x1r),
                                 (sRR[rs_], c_x0r)] if eh == 0 else []
                            w += W(ST3["xst"][rs_]) if eh == 0 else []
                            vop("scalar_tensor_tensor", xs_[:, eh * 512:(eh + 1) * 512],
                                PS[4 + eh][:, :], 1.0,
                                dacc[:, (th * 8 + tk) * 1024 + eh * 512:
                                     (th * 8 + tk) * 1024 + (eh + 1) * 512],
                                ALU.mult, ALU.add, waits=w)
                            vop("tensor_scalar", xs_[:, eh * 512:(eh + 1) * 512],
                                xs_[:, eh * 512:(eh + 1) * 512],
                                stat[:, 48 + tk:49 + tk], None, ALU.mult)
                            vop("tensor_tensor", xs_[:, eh * 512:(eh + 1) * 512],
                                xs_[:, eh * 512:(eh + 1) * 512],
                                x1ar[:, rs_ * 1024 + eh * 512: rs_ * 1024 + (eh + 1) * 512],
                                ALU.add)
                            c_fx = vop("tensor_tensor", xs_[:, eh * 512:(eh + 1) * 512],
                                       xs_[:, eh * 512:(eh + 1) * 512],
                                       x0r[:, rs_ * 1024 + eh * 512: rs_ * 1024 + (eh + 1) * 512],
                                       ALU.add, inc=s_v)
                            ST3["pd"][eh] = (s_v, c_fx)
                        ST3["x1r"][rs_] = (s_v, c_fx)
                        ST3["x0r"][rs_] = (s_v, c_fx)
                        cw = S.dma("sync", x1_s[b, th * 1024 + tk * 128:
                                               th * 1024 + (tk + 1) * 128, :],
                                   xs_, sXW, waits=[(s_v, c_fx)])
                        ST3["xst"][rs_] = (sXW, cw)
                        c_x1_w[(b, th, tk)] = cw
                        if b == 0 and th == 0 and tk == 0:
                            tap("x10", xs_, [(s_v, c_fx)])
                ST3["hidg"] = (s_pe, c_dn)
                ST3["wdg"] = (s_pe, c_dn)
        if b == 0:
            while wq_state["i"] < len(wq_jobs) or wq_state["pend"] is not None:
                emit_wq_job_p2()
            c_p2a_b0_pe[0] = s_pe.n
    while PD0["gen"] is not None or PD0["n"] < 8:
        pair_drv(PD0, 0, True, 8)
    c_wqT_w = sWN.n
    c_p2a_pe = s_pe.n

    AL.off = P2_BASE
    zqfm = AL.bf(8 * 2048)
    wqnr = AL.bf(4 * 4096)
    hbuf = AL.bf(4 * 512)
    qAN = AL.bf(1024)
    qA = AL.bf(1024)
    qC1 = AL.bf(1024)
    qT7 = AL.bf(1024)
    x1r2 = AL.f32(2 * 1024)
    hqg = AL.bf(8 * 2048)
    wqdg = AL.bf(8 * 1024)
    dacc2 = AL.bf(16 * 1024)
    xst2 = AL.f32(2 * 1024)
    assert AL.off <= ARENA_COLS, AL.off

    c_p2a_v = s_v.n
    c_p2a_sc = s_sc.n
    BAR2 = ([(s_pe, c_p2a_pe), (s_v, c_p2a_v), (s_sc, c_p2a_sc)] +
            [(q, q.n) for q in (sRR[0], sRR[1], sXW, sXH[0], sXH[1], sTb[0], sTb[1],
                                sGA[0], sGA[1], sGX, sWN)])
    ST4 = {"x1r": [None, None],
           "wq": [[None, None], [None, None]], "pg": [None, None],
           "pu": [None, None],
           "sgb": [None, None], "hbuf": [None, None], "zqfm": None,
           "hqg": None, "wqdg": None,
           "pd": [[None, None], [None, None]], "xst": [None, None]}
    c_hqf_fg_b = [{}, {}]
    sgb = qsc2[:, 512:1024]
    sgb2 = qsc2[:, 0:512]
    c_out_w = 0

    def gu_gen(b):
        c_gmm = c_umm = None
        for nr in range(6):
            nw = 512 if nr < 5 else 256
            sl2 = nr % 2
            cwg = S.dma("gpsimd",
                        wqnr[:, sl2 * 4096: sl2 * 4096 + 8 * nw]
                        .rearrange("p (et n) -> p et n", n=nw),
                        wqgT_s[:, nr * 512: nr * 512 + nw]
                        .rearrange("(et p) n -> p et n", p=128),
                        sGB[sl2], waits=W(ST4["wq"][0][sl2]) +
                        (([(sWN, c_wqT_w)] + BAR2) if nr == 0 and b == 0 else []))
            cwu = S.dma("gpsimd",
                        wqnr[:, 8192 + sl2 * 4096: 8192 + sl2 * 4096 + 8 * nw]
                        .rearrange("p (et n) -> p et n", n=nw),
                        wquT_s[:, nr * 512: nr * 512 + nw]
                        .rearrange("(et p) n -> p et n", p=128),
                        sGB[sl2], waits=W(ST4["wq"][1][sl2]))
            for tt in range(16):
                par = tt % 2
                first0 = (b == 0 and nr == 0 and tt < 2)
                pg_ap = PSN[:, par * 512: par * 512 + nw]
                pu_ap = PS[par][:, 0:nw]
                sg_ap = (sgb if par == 0 else sgb2)[:, 0:nw]
                hb = hbuf[:, (tt % 4) * 512: (tt % 4) * 512 + nw]
                for et in range(NE):
                    w = []
                    if et == 0:
                        w = W(ST4["pg"][par])
                        if tt == 0:
                            w.append((sGB[sl2], cwg))
                        if nr == 0:
                            w.append((sTb[b], c_zqf_t_b[b][tt // 4]))
                        if first0:
                            w.append((s_sc, c_p2a_sc))
                    c_gmm = mm(pg_ap,
                               zqfm[:, et * 2048 + tt * 128: et * 2048 + tt * 128 + 128],
                               wqnr[:, sl2 * 4096 + et * nw: sl2 * 4096 + (et + 1) * nw],
                               start=(et == 0), stop=(et == NE - 1), waits=w,
                               inc=(s_pe if et == NE - 1 else None))
                for et in range(NE):
                    w = []
                    if et == 0:
                        w = W(ST4["pu"][par])
                        if tt == 0:
                            w.append((sGB[sl2], cwu))
                        if first0:
                            w.append((s_v, c_p2a_v))
                    c_umm = mm(pu_ap,
                               zqfm[:, et * 2048 + tt * 128: et * 2048 + tt * 128 + 128],
                               wqnr[:, 8192 + sl2 * 4096 + et * nw:
                                    8192 + sl2 * 4096 + (et + 1) * nw],
                               start=(et == 0), stop=(et == NE - 1), waits=w,
                               inc=(s_pe if et == NE - 1 else None))
                c_si = vop("activation", sg_ap, pg_ap, AF.Silu,
                           eng="scalar",
                           waits=[(s_pe, c_gmm)] + W(ST4["sgb"][par]) +
                           ([(s_v, c_p2a_v)] if first0 else []), inc=s_sc)
                ST4["pg"][par] = (s_sc, c_si)
                c_h = vop("tensor_tensor", hb, sg_ap, pu_ap, ALU.mult,
                          waits=[(s_sc, c_si), (s_pe, c_umm)] +
                          W(ST4["hbuf"][(tt // 2) % 2]), inc=s_v)
                ST4["pu"][par] = (s_v, c_h)
                ST4["sgb"][par] = (s_v, c_h)
                if nw == 512:
                    if par == 1:
                        xin2 = hbuf[:, (tt - 1) % 4 * 512: ((tt - 1) % 4 + 2) * 512]
                        cq = quant_ops(xin2, 1024, qA, qC1, qT7, qAN,
                                       waits=[(s_v, c_h)])
                        cw = S.dma("sync",
                                   hq_b[b][(tt - 1) * 128:(tt + 1) * 128,
                                           nr * 512: nr * 512 + 512]
                                   .rearrange("(t p) n -> p t n", p=128),
                                   xin2.rearrange("p (t n) -> p t n", n=512),
                                   sWQ[b], waits=[(s_v, cq)])
                        ST4["hbuf"][(tt // 2) % 2] = (sWQ[b], cw)
                else:
                    cq = quant_ops(hb, nw, qA, qC1, qT7, qAN,
                                   waits=[(s_v, c_h)])
                    cw = S.dma("sync",
                               hq_b[b][tt * 128:(tt + 1) * 128,
                                       nr * 512: nr * 512 + nw],
                               hb, sWQ[b], waits=[(s_v, cq)])
                    ST4["hbuf"][(tt // 2) % 2] = (sWQ[b], cw)
                yield
            ST4["wq"][0][sl2] = (s_pe, c_gmm)
            ST4["wq"][1][sl2] = (s_pe, c_umm)
            if nr == 1 and b == 0:
                c_hqf = None
                for j_ in range(8):
                    w = ([(sWQ[b], 0)] if j_ == 0 else []) + \
                        (W(ST4["hqg"]) if j_ == 0 else [])
                    c_hqf = S.dmat("scalar", hqg[:, j_ * 2048:(j_ + 1) * 2048],
                                   hq_b[b][:, j_ * 128:(j_ + 1) * 128], sTb[b],
                                   waits=w)
                c_hqf_fg_b[0][0] = c_hqf
        ST4["zqfm"] = (s_pe, s_pe.n)

    def down_gen(b):
        for fg in range(3):
            gsz = 8 if fg < 2 else 6
            for j in range(gsz):
                ft = fg * 8 + j
                if fg > 0 or b == 1:
                    w = ([(sWQ[b], 0)] if j == 0 else []) + \
                        (W(ST4["hqg"]) if j == 0 else [])
                    c_hqf_fg_b[b][fg] = S.dmat(
                        "scalar", hqg[:, j * 2048:(j + 1) * 2048],
                        hq_b[b][:, ft * 128:(ft + 1) * 128], sTb[b], waits=w)
                c_wd2 = S.dma("gpsimd", wqdg[:, j * 1024:(j + 1) * 1024],
                              wqdT_s[ft * 128:(ft + 1) * 128, :], sGD,
                              waits=W(ST4["wqdg"]) if j == 0 else ())
            for tk in range(16):
                rs_ = tk % 2
                pbk = (PS[4], PS[5]) if rs_ == 0 else (PS[2], PS[3])
                if fg == 2:
                    c_xr2 = S.dma("sync", x1r2[:, rs_ * 1024:(rs_ + 1) * 1024],
                                  x1_s[b, tk * 128:(tk + 1) * 128, :], sRR[rs_],
                                  waits=W(ST4["x1r"][rs_]) +
                                  ([(s_v, c_p2a_v)]
                                   if b == 0 and tk < 2 else []))
                c_dn = None
                for j in range(gsz):
                    for eh in range(2):
                        w = []
                        if j == 0 and eh == 0:
                            w = W(ST4["pd"][rs_][0], ST4["pd"][rs_][1])
                            if tk == 0:
                                w += [(sTb[b], c_hqf_fg_b[b][fg]), (sGD, c_wd2)]
                            if b == 0 and fg == 0 and tk < 2:
                                w += [(s_v, c_p2a_v), (s_sc, c_p2a_sc)]
                        c_dn = mm(pbk[eh][:, :],
                                  hqg[:, j * 2048 + tk * 128: j * 2048 + tk * 128 + 128],
                                  wqdg[:, j * 1024 + eh * 512: j * 1024 + (eh + 1) * 512],
                                  start=(j == 0), stop=(j == gsz - 1), waits=w,
                                  inc=(s_pe if j == gsz - 1 else None))
                if fg < 2:
                    for eh in range(2):
                        if fg == 0:
                            c_ev = vop("tensor_copy",
                                       dacc2[:, tk * 1024 + eh * 512: tk * 1024 + (eh + 1) * 512],
                                       pbk[eh][:, :],
                                       waits=[(s_pe, c_dn)] if eh == 0 else (), inc=s_v)
                        else:
                            c_ev = vop("scalar_tensor_tensor",
                                       dacc2[:, tk * 1024 + eh * 512: tk * 1024 + (eh + 1) * 512],
                                       pbk[eh][:, :], 1.0,
                                       dacc2[:, tk * 1024 + eh * 512: tk * 1024 + (eh + 1) * 512],
                                       ALU.mult, ALU.add,
                                       waits=[(s_pe, c_dn)] if eh == 0 else (), inc=s_v)
                        ST4["pd"][rs_][eh] = (s_v, c_ev)
                else:
                    xs_ = xst2[:, rs_ * 1024:(rs_ + 1) * 1024]
                    for eh in range(2):
                        w = ([(s_pe, c_dn), (sRR[rs_], c_xr2)] + W(ST4["xst"][rs_])) \
                            if eh == 0 else []
                        vop("scalar_tensor_tensor", xs_[:, eh * 512:(eh + 1) * 512],
                            pbk[eh][:, :], 1.0,
                            dacc2[:, tk * 1024 + eh * 512: tk * 1024 + (eh + 1) * 512],
                            ALU.mult, ALU.add, waits=w)
                        c_fx = vop("tensor_tensor", xs_[:, eh * 512:(eh + 1) * 512],
                                   xs_[:, eh * 512:(eh + 1) * 512],
                                   x1r2[:, rs_ * 1024 + eh * 512: rs_ * 1024 + (eh + 1) * 512],
                                   ALU.add, inc=s_v)
                        ST4["pd"][rs_][eh] = (s_v, c_fx)
                    ST4["x1r"][rs_] = (s_v, c_fx)
                    cow = S.dma("sync", out_d[b, tk * 128:(tk + 1) * 128, :],
                                xs_, sOut, waits=[(s_v, c_fx)])
                    ST4["xst"][rs_] = (sOut, cow)
                yield
            ST4["hqg"] = (s_pe, c_dn)
            ST4["wqdg"] = (s_pe, c_dn)

    PD1 = {"gen": None, "n": 0}
    u = 0
    for _ in gu_gen(0):
        if u % 2 == 0 or PD1["gen"] is None:
            pair_drv(PD1, 1, u % 12 == 4, 8)
        u += 1
    while PD1["gen"] is not None or PD1["n"] < 8:
        pair_drv(PD1, 1, True, 8)
    c_zqf = None
    for et in range(NE):
        w = [(sWQ[1], 0), (s_pe, s_pe.n)] if et == 0 else []
        c_zqf = S.dmat("scalar", zqfm[:, et * 2048:(et + 1) * 2048],
                       zq2_s[:, et * 128:(et + 1) * 128], sTb[1], waits=w)
    for g_ in range(4):
        c_zqf_t_b[1][g_] = c_zqf
    g1 = gu_gen(1)
    d0 = down_gen(0)
    alive = True
    while alive:
        alive = False
        for _ in range(2):
            if next(g1, "end") != "end":
                alive = True
        if next(d0, "end") != "end":
            alive = True
    for _ in down_gen(1):
        pass
    S.op("sync", lambda e: e.wait_ge(sOut.h, sOut.n))

    with nc.Block() as block:
        def mk(name):
            def run(eng):
                for (fn, waits, inc, amt) in S.q[name]:
                    for (sw, v) in waits:
                        eng.wait_ge(sw.h, v)
                    inst = fn(eng)
                    if inc is not None and inst is not None:
                        inst.then_inc(inc.h, amt)
            return run
        block.sync(mk("sync"))
        block.gpsimd(mk("gpsimd"))
        block.vector(mk("vector"))
        block.scalar(mk("scalar"))
        block.tensor(mk("tensor"))
    ctx.close()
    return nc


TAP_SPECS = {
    "y0": ([128, 1024], F32), "y1": ([128, 1024], F32),
    "y2": ([128, 1024], F32), "y3": ([128, 1024], F32),
    "qkv0": ([128, 2048], BF16), "qrope0": ([128, 1024], BF16),
    "attn0": ([128, 1024], BF16), "x1a0": ([128, 1024], F32),
    "h20": ([128, 1024], BF16), "sizes": ([1, 32], F32),
    "x10": ([128, 1024], F32), "zq0": ([128, 1024], BF16),
}


def kernel(L0_attn_x, L0_attn_w_qkv, L0_attn_w_out, L0_attn_g1, L0_attn_g2,
           L0_attn_w_dp1, L0_attn_w_dp2, L0_attn_w_gate, L0_attn_w_up,
           L0_attn_w_down, L0_attn_g_hidden, L0_ffn_w_gate, L0_ffn_w_up,
           L0_ffn_w_down):
    x = np.asarray(L0_attn_x, dtype=np.float32)
    wqkv = np.asarray(L0_attn_w_qkv, dtype=np.float32)
    wout = np.asarray(L0_attn_w_out, dtype=np.float32)
    g1 = np.asarray(L0_attn_g1, dtype=np.float32).reshape(1, E)
    g2 = np.asarray(L0_attn_g2, dtype=np.float32).reshape(1, E)
    wdp1 = np.asarray(L0_attn_w_dp1, dtype=np.float32)
    wdp2 = np.asarray(L0_attn_w_dp2, dtype=np.float32).reshape(1, 256)
    wga = np.asarray(L0_attn_w_gate, dtype=np.float32)
    wua = np.asarray(L0_attn_w_up, dtype=np.float32)
    wda = np.asarray(L0_attn_w_down, dtype=np.float32)
    gh = np.asarray(L0_attn_g_hidden, dtype=np.float32).reshape(1, FA)
    wqg = np.asarray(L0_ffn_w_gate, dtype=np.float32)
    wqu = np.asarray(L0_ffn_w_up, dtype=np.float32)
    wqd = np.asarray(L0_ffn_w_down, dtype=np.float32)

    ce, co, se, so, tri, iota_col, ident = host_constants()
    tabs = np.concatenate([ce, co, se, so], axis=0)
    pm = _pm(0)
    perm = np.arange(3 * E)
    for h in range(H):
        for i in range(D):
            perm[64 * h + i] = 64 * h + pm[i]
            perm[E + 64 * h + i] = E + 64 * h + pm[i]
    wqkv_p = np.ascontiguousarray(wqkv[:, perm])

    taps = [(nm,) + TAP_SPECS[nm] for nm in DEBUG_TAPS]
    nc = build_program(taps=taps)
    in_maps = []
    for c in range(8):
        in_maps.append({
            "x": np.ascontiguousarray(x[2 * c:2 * c + 2]),
            "wqkv": wqkv_p, "wout": wout, "g1": g1, "g2": g2,
            "wdp1": wdp1, "wdp2": wdp2, "wga": wga, "wua": wua, "wda": wda,
            "gh": gh, "wqg": wqg, "wqu": wqu, "wqd": wqd,
            "tabs": tabs, "trim": np.asarray(tri),
            "iotac": iota_col, "ident": np.asarray(ident),
        })
    res = run_bass_kernel_spmd(nc, in_maps, core_ids=list(range(8)))
    out = np.concatenate([np.asarray(r["out"]) for r in res.results], axis=0)
    if DEBUG_TAPS:
        kernel.tap_results = res.results
    return out


# version 24
# speedup vs baseline: 1.0152x; 1.0152x over previous
import math
import numpy as np
import ml_dtypes
import concourse.bass as bass
import concourse.mybir as mybir
from concourse.bass_utils import run_bass_kernel_spmd
from contextlib import ExitStack

F32 = mybir.dt.float32
BF16 = mybir.dt.bfloat16
I32 = mybir.dt.int32
U16 = mybir.dt.uint16
AF = mybir.ActivationFunctionType
ALU = mybir.AluOpType
AXX = mybir.AxisListType.X

B = 2
L = 2048
E = 1024
H = 16
D = 64
FA = 4608
FI = 2816
CH = 512
NT = 4
NCH = L // CH
NE = E // 128
NFA = FA // 128
NFI = FI // 128

ENGS = ("sync", "gpsimd", "vector", "scalar", "tensor")

DEBUG_TAPS = ()


class Semw:
    __slots__ = ("h", "n", "name")

    def __init__(self, h, name):
        self.h = h
        self.n = 0
        self.name = name


class Sched:
    def __init__(self):
        self.q = {e: [] for e in ENGS}

    DMA_SEMS = ()

    def op(self, eng, fn, waits=(), inc=None, amt=1):
        w = tuple((s, (s.n if s in Sched.DMA_SEMS else v)) for (s, v) in waits)
        self.q[eng].append((fn, w, inc, amt))
        if inc is not None:
            inc.n += amt
            return inc.n
        return None

    def dma(self, eng, out, in_, sem, waits=(), **kw):
        return self.op(eng, lambda e: e.dma_start(out=out, in_=in_, **kw),
                       waits=waits, inc=sem, amt=16)

    def dmat(self, eng, out, in_, sem, waits=()):
        return self.op(eng, lambda e: e.dma_start(out=out, in_=in_, transpose=True),
                       waits=waits, inc=sem, amt=16)


def _pm(h):
    return [2 * i for i in range(32)] + [2 * i + 1 for i in range(32)]


def host_constants():
    pos = np.arange(L, dtype=np.float64)
    inv = 1.0 / (10000.0 ** (np.arange(0, D, 2, dtype=np.float64) / D))
    fr = pos[:, None] * inv[None, :]
    emb = np.concatenate([fr, fr], axis=1)
    cos = np.cos(emb)
    sin = np.sin(emb)
    ce = cos[:, 0::2].astype(np.float32)
    co = cos[:, 1::2].astype(np.float32)
    se = sin[:, 0::2].astype(np.float32)
    so = sin[:, 1::2].astype(np.float32)
    tri = (np.arange(128)[:, None] <= np.arange(128)[None, :]).astype(np.float32)
    iota_col = np.arange(FA, dtype=np.float32).reshape(NFA, 128).T.copy()
    ident = np.eye(128, dtype=np.float32)
    return (ce, co, se, so, tri.astype(ml_dtypes.bfloat16), iota_col,
            ident.astype(ml_dtypes.bfloat16))


def build_program(taps=()):
    nc = bass.Bass("TRN2", target_bir_lowering=False,
                   detect_race_conditions=False)
    S = Sched()

    def din(name, shape, dt=F32):
        return nc.dram_tensor(name, shape, dt, kind="ExternalInput").ap()

    x_in = din("x", [B, L, E])
    wqkv_in = din("wqkv", [E, 3 * E])
    wout_in = din("wout", [E, E])
    g1_in = din("g1", [1, E])
    g2_in = din("g2", [1, E])
    wdp1_in = din("wdp1", [E, 256])
    wdp2_in = din("wdp2", [1, 256])
    wga_in = din("wga", [E, FA])
    wua_in = din("wua", [E, FA])
    wda_in = din("wda", [FA, E])
    gh_in = din("gh", [1, FA])
    wqg_in = din("wqg", [FI, E])
    wqu_in = din("wqu", [FI, E])
    wqd_in = din("wqd", [E, FI])
    tab_in = din("tabs", [4 * L, 32])
    trim_in = din("trim", [128, 128], BF16)
    iota_in = din("iotac", [128, NFA])
    ident_in = din("ident", [128, 128], BF16)

    out_d = nc.dram_tensor("out", [B, L, E], F32, kind="ExternalOutput").ap()

    tap_d = {}
    for t in taps:
        nm, shape, dt = t
        tap_d[nm] = nc.dram_tensor("tap_" + nm, shape, dt, kind="ExternalOutput").ap()

    def dscr(name, shape, dt):
        return nc.dram_tensor(name, shape, dt, kind="Internal").ap()

    h2_s = dscr("h2_s", [B, L, E], BF16)
    x1a_s = dscr("x1a_s", [B, L, E], F32)
    x1_s = dscr("x1_s", [B, L, E], F32)
    yrt_s = dscr("yrt_s", [2, CH, E], BF16)
    qrt_s = dscr("qrt_s", [2, CH, E], BF16)
    krt_s = dscr("krt_s", [2, CH, E], BF16)
    art_s = dscr("art_s", [2, CH, E], BF16)
    hid_s = dscr("hid_s", [L, FA], BF16)
    zq_s = dscr("zq_s", [L, E], BF16)
    zq2_s = dscr("zq2_s", [L, E], BF16)
    hq_s = dscr("hq_s", [L, FI], BF16)
    hq2_s = dscr("hq2_s", [L, FI], BF16)
    zq_b = [zq_s, zq2_s]
    hq_b = [hq_s, hq2_s]
    wqgT_s = dscr("wqgT_s", [E, FI], BF16)
    wquT_s = dscr("wquT_s", [E, FI], BF16)
    wqdT_s = dscr("wqdT_s", [FI, E], BF16)
    cc_in = nc.dram_tensor("cc_in", [1, 8], F32, kind="Internal")
    cc_out = nc.dram_tensor("cc_out", [1, 8], F32, kind="Internal")

    ctx = ExitStack()
    Ea = ctx.enter_context

    ARENA_COLS = 100992
    arena = Ea(nc.sbuf_tensor("arena", [128, ARENA_COLS], BF16))

    class Alloc:
        def __init__(self):
            self.off = 0
            self.marks = {}

        def bf(self, cols):
            a = self.off
            assert a + cols <= ARENA_COLS, f"arena overflow {a + cols}"
            self.off += (cols + 1) & ~1
            return arena[:, a:a + cols]

        def f32(self, elems):
            return self.bf(2 * elems).bitcast(F32)

    AL = Alloc()
    tab_ce = AL.f32(16 * 32)
    tab_co = AL.f32(16 * 32)
    tab_se = AL.f32(16 * 32)
    tab_so = AL.f32(16 * 32)
    P2_BASE = AL.off

    g1c = Ea(nc.sbuf_tensor("g1c", [128, NE], F32))
    g2c = Ea(nc.sbuf_tensor("g2c", [128, NE], F32))
    ghc = Ea(nc.sbuf_tensor("ghc", [128, NFA], F32))
    mgc = Ea(nc.sbuf_tensor("mgc", [128, NFA], F32))
    mgcb = Ea(nc.sbuf_tensor("mgcb", [128, NFA], BF16))
    iotac = Ea(nc.sbuf_tensor("iotac_sb", [128, NFA], F32))
    trim = Ea(nc.sbuf_tensor("trim_sb", [128, 128], BF16))
    idb = Ea(nc.sbuf_tensor("idb", [128, 128], BF16))
    wdp2r = Ea(nc.sbuf_tensor("wdp2r", [1, 256], F32))
    onesb = Ea(nc.sbuf_tensor("onesb", [128, 8], BF16))
    ones1b = Ea(nc.sbuf_tensor("ones1b", [1, 128], BF16))
    ones1f = Ea(nc.sbuf_tensor("ones1f", [1, 128], F32))
    stat = Ea(nc.sbuf_tensor("stat", [128, 160], F32))
    ssc = Ea(nc.sbuf_tensor("ssc", [128, 34], F32))
    sdp = Ea(nc.sbuf_tensor("sdp", [1, 1104], F32))
    sdpb = Ea(nc.sbuf_tensor("sdpb", [128, 16], BF16))
    s_sb = Ea(nc.sbuf_tensor("s_sb", [128, 528], BF16))
    qs = Ea(nc.sbuf_tensor("qs", [128, 192], F32))
    qsb = Ea(nc.sbuf_tensor("qsb", [128, 128], BF16))

    PS = [Ea(nc.psum_tensor(f"PS{i}", [128, 512], F32)) for i in range(6)]
    PSN = Ea(nc.psum_tensor("PSN", [128, 1024], F32))

    def sem(name):
        return Semw(Ea(nc.semaphore(name)), name)

    s_v = sem("s_v")
    s_sc = sem("s_sc")
    s_pe = sem("s_pe")
    s_cc = sem("s_cc")
    sCON = sem("sCON")
    sX0 = sem("sX0")
    sYW = [sem("sYW0"), sem("sYW1")]
    sQW = [sem("sQW0"), sem("sQW1")]
    sAW = [sem("sAW0"), sem("sAW1")]
    sXH = [sem("sXH0"), sem("sXH1")]
    sXW = sem("sXW")
    sRR = [sem("sRR0"), sem("sRR1")]
    sOut = sem("sOut")
    sWQ = [sem("sWQ0"), sem("sWQ1")]
    sWN = sem("sWN")
    sGW = sem("sGW")
    sGC = [sem("sGC0"), sem("sGC1")]
    sGA = [sem("sGA0"), sem("sGA1")]
    sGB = [sem("sGB0"), sem("sGB1")]
    sGD = sem("sGD")
    sGX = sem("sGX")
    sTa = sem("sTa")
    sTb = [sem("sTb0"), sem("sTb1")]
    Sched.DMA_SEMS = (sCON, sX0, sYW[0], sYW[1], sQW[0], sQW[1], sAW[0], sAW[1],
                      sXH[0], sXH[1], sXW, sRR[0], sRR[1], sOut, sWQ[0], sWQ[1],
                      sWN, sGW, sGC[0], sGC[1], sGA[0], sGA[1], sGB[0], sGB[1],
                      sGD, sGX, sTa, sTb[0], sTb[1])

    def mm(ps_ap, lhsT, rhs, start, stop, waits=(), inc=None, tp=None,
           skip_group_check=False):
        return S.op("tensor",
                    lambda e, o=ps_ap, l=lhsT, r=rhs, st=start, sp=stop, t=tp,
                    sg=skip_group_check:
                    e.matmul(o, l, r, start=st, stop=sp, tile_position=t,
                             skip_group_check=sg),
                    waits=waits, inc=inc)

    def vop(meth, *args, waits=(), inc=None, eng="vector", **kw):
        def fn(e, m=meth, a=args, k=kw):
            return getattr(e, m)(*a, **k)
        return S.op(eng, fn, waits=waits, inc=inc)

    def vops(meth, *args, waits=(), **kw):
        w = list(waits)
        if s_v.n > 0:
            w.append((s_v, s_v.n))
        return vop(meth, *args, waits=w, inc=s_v, **kw)

    def wsmall():
        return [(s_v, s_v.n)] if s_v.n > 0 else []

    def tap(nm, src_ap, waits):
        if nm in tap_d:
            S.dma("sync", tap_d[nm][:, :] if tap_d[nm].ndim == 2 else tap_d[nm],
                  src_ap, sOut, waits=waits)

    def newton_rsqrt(out_ap, v_ap, eps, w):
        t0 = stat[:, 128:128 + w]
        t1 = stat[:, 144:144 + w]
        assert w <= 16
        vops("tensor_scalar", t0, v_ap, float(eps), None, ALU.add)
        t0i = t0.bitcast(I32)
        t1i = t1.bitcast(I32)
        vops("tensor_copy", out_ap, t0i)
        vops("tensor_scalar", out_ap, out_ap, -0.5, float(0x5F3759DF),
             ALU.mult, ALU.add)
        vops("tensor_copy", t1i, out_ap)
        vops("tensor_copy", out_ap, t1)
        c = None
        for it in range(2):
            vops("tensor_tensor", t1, out_ap, out_ap, ALU.mult)
            vops("tensor_tensor", t1, t1, t0, ALU.mult)
            vops("tensor_scalar", t1, t1, -0.5, 1.5, ALU.mult, ALU.add)
            c = vops("tensor_tensor", out_ap, out_ap, t1, ALU.mult)
        return c

    wdp1_sb = AL.bf(NE * 256)
    P2_BASE = AL.off
    wqkv_sb = AL.bf(NE * 3072)
    wout_sb = AL.bf(NE * 1024)
    x0c = AL.bf(NT * 1024)
    ypb = AL.bf(1024)
    ybk = AL.bf(2 * NT * 1024)
    yfm = AL.bf(2 * NE * 512)
    qktm = AL.bf(2 * 2048)
    qkr = AL.bf(NT * 1024)
    kkr = AL.bf(NT * 1024)
    rsc = AL.bf(1536)
    qfm = AL.bf(8 * 512)
    kfm = AL.bf(8 * 512)
    vaug = AL.bf(NT * 1040)
    atm = AL.bf(4 * 128)
    attn_tm = AL.bf(NT * 1024)
    evst2 = attn_tm[:, 0:2048].bitcast(F32)
    attnfm = AL.bf(NE * 512)
    evst = AL.f32(1024)
    h2c = AL.bf(2 * 1024)
    assert AL.off <= ARENA_COLS, AL.off

    tabv = [tab_ce, tab_co, tab_se, tab_so]
    for i in range(4):
        S.dma("sync", tabv[i].rearrange("p (lt i) -> p lt i", i=32),
              tab_in[i * L:(i + 1) * L, :].rearrange("(lt p) i -> p lt i", p=128),
              sCON)
    S.dma("sync", trim[:, :], trim_in[:, :], sCON)
    S.dma("sync", iotac[:, :], iota_in[:, :], sCON)
    S.dma("sync", idb[:, :], ident_in[:, :], sCON)
    S.dma("sync", g1c[:, :], g1_in.rearrange("o (et p) -> (o p) et", p=128), sCON,
          allow_slow_non_contiguous=True)
    S.dma("sync", g2c[:, :], g2_in.rearrange("o (et p) -> (o p) et", p=128), sCON,
          allow_slow_non_contiguous=True)
    S.dma("sync", ghc[:, :], gh_in.rearrange("o (ft p) -> (o p) ft", p=128), sCON,
          allow_slow_non_contiguous=True)
    S.dma("sync", wdp2r[:, :], wdp2_in[:, :], sCON)
    c_const = sCON.n

    c_wqkv = S.dma("gpsimd", wqkv_sb.rearrange("p (et f) -> p et f", f=3 * E),
                   wqkv_in.rearrange("(et p) f -> p et f", p=128), sGW)
    c_wout = S.dma("gpsimd", wout_sb.rearrange("p (et f) -> p et f", f=E),
                   wout_in.rearrange("(et p) f -> p et f", p=128), sGW)
    c_wdp1 = S.dma("gpsimd", wdp1_sb.rearrange("p (et f) -> p et f", f=256),
                   wdp1_in.rearrange("(et p) f -> p et f", p=128), sGW)

    vops("memset", onesb[:, :], 1.0)
    vops("memset", ones1b[:, :], 1.0)
    c_ones1 = vops("memset", ones1f[:, :], 1.0)
    c_ones = vop("memset",
                 vaug.rearrange("p (t h c) -> p (t h) c", c=65, h=16)[:, :, 64:65], 1.0,
                 inc=s_v)
    for et in range(NE):
        w = [(sGW, c_wqkv), (sCON, c_const)] if et == 0 else ()
        vop("tensor_scalar", wqkv_sb[:, et * 3072:(et + 1) * 3072],
            wqkv_sb[:, et * 3072:(et + 1) * 3072], g1c[:, et:et + 1], None, ALU.mult,
            waits=w)
    for et in range(NE):
        w = [(sGW, c_wdp1)] if et == 0 else ()
        c_wfold = vop("tensor_scalar", wdp1_sb[:, et * 256:(et + 1) * 256],
                      wdp1_sb[:, et * 256:(et + 1) * 256], g2c[:, et:et + 1], None,
                      ALU.mult, waits=w, inc=s_v)

    wqx = AL.bf(2 * 512)
    wqa = AL.bf(512)
    wqxT = qkr
    assert AL.off <= ARENA_COLS, AL.off

    def quant_ops(xin, W_, a_buf, c1_buf, t7_buf, an_buf, waits=(), sc_waits=()):
        nb = W_ // 16
        Mf = qs[:, 0:nb]
        sf = qs[:, 64:64 + nb]
        rsf = qs[:, 128:128 + nb]
        vops("tensor_reduce", Mf.rearrange("p (g o) -> p g o", o=1),
             xin.rearrange("p (g z) -> p g z", z=16), AXX, ALU.max,
             apply_absolute_value=True, waits=waits)
        vops("tensor_scalar", sf, Mf, 1.0 / 6.0, 1e-12, ALU.mult, ALU.max)
        vops("reciprocal", rsf, sf)
        an = an_buf[:, 0:W_]
        c_an = vop("tensor_tensor", an.rearrange("p (g z) -> p g z", z=16),
                   xin.rearrange("p (g z) -> p g z", z=16),
                   rsf.rearrange("p (g o) -> p g o", o=1).broadcast_to([128, nb, 16]),
                   ALU.mult, waits=wsmall(), inc=s_v)
        a_ = a_buf[:, 0:W_]
        c_abs = vop("activation", a_, an, AF.Abs, eng="scalar",
                    waits=[(s_v, c_an)] + list(sc_waits), inc=s_sc)
        c1 = c1_buf[:, 0:W_]
        t7 = t7_buf[:, 0:W_]
        vop("tensor_scalar", c1, a_, 0.25, None, ALU.is_gt, waits=[(s_sc, c_abs)])
        vop("scalar_tensor_tensor", c1, a_, 0.75, c1, ALU.is_gt, ALU.add)
        vop("scalar_tensor_tensor", c1, a_, 1.25, c1, ALU.is_gt, ALU.add)
        vop("scalar_tensor_tensor", c1, a_, 1.75, c1, ALU.is_gt, ALU.add)
        vop("tensor_scalar", t7, a_, 5.0, 2.0, ALU.is_gt, ALU.mult)
        vop("scalar_tensor_tensor", t7, a_, 3.5, t7, ALU.is_gt, ALU.add)
        vop("scalar_tensor_tensor", t7, a_, 2.5, t7, ALU.is_gt, ALU.add)
        vop("scalar_tensor_tensor", c1, c1, 0.5, t7, ALU.mult, ALU.add)
        vop("tensor_tensor", c1.rearrange("p (g z) -> p g z", z=16),
            c1.rearrange("p (g z) -> p g z", z=16),
            sf.rearrange("p (g o) -> p g o", o=1).broadcast_to([128, nb, 16]),
            ALU.mult)
        vop("tensor_scalar", a_.bitcast(U16), an.bitcast(U16), 0x8000, None,
            ALU.bitwise_and)
        return vop("tensor_tensor", xin.bitcast(U16), c1.bitcast(U16),
                   a_.bitcast(U16), ALU.bitwise_or, inc=s_v)

    wq_jobs = []
    for ti in range(NFI):
        for ks in range(2):
            wq_jobs.append((wqg_in, ti, ks * 512, 512, wqgT_s))
    for ti in range(NFI):
        for ks in range(2):
            wq_jobs.append((wqu_in, ti, ks * 512, 512, wquT_s))
    for ti in range(NE):
        for ks in range(6):
            w_ = 512 if ks < 5 else 256
            wq_jobs.append((wqd_in, ti, ks * 512, w_, wqdT_s))
    wq_state = {"i": 0, "free": [None, None], "tfree": [None, None],
                "pend": None, "ps3": None, "t2free": None,
                "p2first_t": True, "p2first_q": True}

    def emit_wq_jobs(n, a_buf, c1_buf, t7_buf, an_buf):
        for _ in range(n):
            i = wq_state["i"]
            if i >= len(wq_jobs):
                return
            wq_state["i"] = i + 1
            src, ti, ko, w_, dstT = wq_jobs[i]
            sl2 = i % 2
            xin = wqx[:, sl2 * 512: sl2 * 512 + w_]
            cL = S.dma("gpsimd", xin, src[ti * 128:(ti + 1) * 128, ko:ko + w_],
                       sGC[sl2], waits=W(wq_state["free"][sl2]))
            cq = quant_ops(xin, w_, a_buf, c1_buf, t7_buf, an_buf,
                           waits=[(sGC[sl2], cL)], sc_waits=())
            nsub = w_ // 128
            ps = PS[sl2]
            pkey = "psA" if sl2 == 0 else "psB"
            c_T = None
            for t in range(nsub):
                w = []
                if t == 0:
                    w = [(s_v, cq)] + W(ST[pkey])
                    if i == 0:
                        w.append((sCON, c_const))
                c_T = mm(ps[:, t * 128:(t + 1) * 128],
                         xin[:, t * 128:(t + 1) * 128], idb[:, :],
                         start=True, stop=True, waits=w,
                         inc=(s_pe if t == nsub - 1 else None))
            wq_state["free"][sl2] = (s_pe, c_T)
            xt = wqxT[:, sl2 * 512: sl2 * 512 + w_]
            wq_ = [(sQW[0], 0), (sQW[1], 0)] if wq_state["tfree"][sl2] is None \
                else []
            c_evT = vop("copy", xt, ps[:, 0:w_], eng="scalar",
                        waits=[(s_pe, c_T)] + W(wq_state["tfree"][sl2]) + wq_,
                        inc=s_sc)
            ST[pkey] = (s_sc, c_evT)
            cw = S.dma("sync",
                       dstT[ko:ko + w_, ti * 128:(ti + 1) * 128]
                       .rearrange("(t p) n -> p t n", p=128),
                       xt.rearrange("p (t n) -> p t n", n=128),
                       sWN, waits=[(s_sc, c_evT)])
            wq_state["tfree"][sl2] = (sWN, cw)

    ST = {"x0c_free": None, "psA": None, "psB": None, "yrt": [None, None],
          "qrt": [None, None], "krt": [None, None], "art": [None, None],
          "ypb": [None, None], "qktm": [None, None], "h2c": [None, None],
          "atm": [None] * 4, "atb": [None, None], "psn": None,
          "evst": [None, None],
          "yfm": [None, None], "ybk": [None, None], "qfm": None, "kfm": None,
          "s_sb": None, "vaug": None, "attnfm": None, "rope_free": None,
          "rope_free2": None, "x0_ld": None, "yfm_rd": [None, None]}

    def W(*keys):
        r = []
        for k in keys:
            v = ST[k] if isinstance(k, str) else k
            if v is not None:
                r.append(v)
        return r

    c_h2_w = {}
    c_x1a_w = {}
    NCHT = B * NCH

    def emit_x0_load(i):
        bb, chh = divmod(i, NCH)
        c = S.dma("gpsimd", x0c.rearrange("p (t e) -> p t e", e=E),
                  x_in[bb, chh * CH:(chh + 1) * CH, :]
                  .rearrange("(t p) e -> p t e", p=128),
                  sX0, waits=W("x0c_free"))
        ST["x0_ld"] = (sX0, c)

    def emit_ln(i):
        bb, chh = divmod(i, NCH)
        sl = i % 2
        for tt in range(NT):
            xs = x0c[:, tt * 1024:(tt + 1) * 1024]
            w0 = W("x0_ld") if tt == 0 else []
            vops("bn_stats", stat[:, tt * 12: tt * 12 + 6], xs[:, 0:512],
                 waits=w0)
            vops("bn_stats", stat[:, tt * 12 + 6: tt * 12 + 12], xs[:, 512:1024])
        g4 = stat[:, 0:48].rearrange("p (t g c) -> p t g c", g=4, c=3)
        means = g4[:, :, :, 1]
        m2s = g4[:, :, :, 2]
        mt = stat[:, 64:68]
        vt = stat[:, 68:72]
        vops("reduce_sum", mt.rearrange("p (t o) -> p t o", o=1), means, AXX)
        vops("tensor_scalar", mt, mt, 0.25, None, ALU.mult)
        vops("reduce_sum", vt.rearrange("p (t o) -> p t o", o=1), m2s, AXX)
        vops("tensor_tensor", stat[:, 72:76], mt, mt, ALU.mult)
        t16 = stat[:, 80:96].rearrange("p (t g) -> p t g", g=4)
        vops("tensor_tensor", t16, means, means, ALU.mult)
        vops("reduce_sum", stat[:, 76:80].rearrange("p (t o) -> p t o", o=1),
             t16, AXX)
        vops("tensor_scalar", stat[:, 76:80], stat[:, 76:80], 256.0, None,
             ALU.mult)
        vops("tensor_tensor", vt, vt, stat[:, 76:80], ALU.add)
        vops("tensor_scalar", vt, vt, 1.0 / 1024.0, None, ALU.mult)
        vops("tensor_tensor", vt, vt, stat[:, 72:76], ALU.subtract)
        a1 = stat[:, 96:100]
        newton_rsqrt(a1, vt, 1e-5, 4)
        vops("tensor_tensor", stat[:, 100:104], a1, a1, ALU.mult)
        vops("tensor_tensor", stat[:, 100:104], stat[:, 100:104], vt, ALU.mult)
        a2 = stat[:, 104:108]
        newton_rsqrt(a2, stat[:, 100:104], 1e-6, 4)
        vops("tensor_scalar", a2, a2, 1e-6, 1e6, ALU.max, ALU.min)
        for tt in range(NT):
            xs = x0c[:, tt * 1024:(tt + 1) * 1024]
            yb = ybk[:, sl * 4096 + tt * 1024: sl * 4096 + (tt + 1) * 1024]
            c_y = vop("tensor_scalar", yb, xs, mt[:, tt:tt + 1],
                      a1[:, tt:tt + 1], ALU.subtract, ALU.mult,
                      waits=wsmall() + (W(ST["ybk"][sl]) if tt == 0 else []),
                      inc=s_v)
            yp = ypb[:, 0:1024]
            c_yp = vop("tensor_scalar", yp, yb, a2[:, tt:tt + 1], None, ALU.mult,
                       waits=W(ST["ypb"][0]), inc=s_v)
            cw = S.dma("sync", yrt_s[sl, tt * 128:(tt + 1) * 128, :], yp, sYW[sl],
                       waits=[(s_v, c_yp)] + (W(ST["yrt"][sl]) if tt == 0 else []))
            ST["ypb"][0] = (sYW[sl], cw)
            if i == 0:
                tap("y" + str(tt), yb, [(s_v, c_y)])
        ST["x0c_free"] = (s_v, s_v.n)
        c_ypb_w = sYW[sl].n
        c_yfm = None
        for et in range(NE):
            w = ([(sYW[sl], c_ypb_w)] if et == 0 else []) + \
                (W(ST["yfm"][sl]) if et == 0 else [])
            c_yfm = S.dmat("scalar", yfm[:, sl * 4096 + et * 512: sl * 4096 + (et + 1) * 512],
                           yrt_s[sl][:, et * 128:(et + 1) * 128], sTa, waits=w)
        ST["yrt"][sl] = (sTa, c_yfm)
        ST["yfm_rd"][sl] = (sTa, c_yfm)

    c_msS = None
    emit_x0_load(0)
    emit_ln(0)
    for i in range(NCHT):
        bb, chh = divmod(i, NCH)
        sl = i % 2
        lt0 = chh * NT
        first_of_b = (chh == 0)
        if i + 1 < NCHT:
            emit_x0_load(i + 1)
        if first_of_b:
            vops("memset", PS[4][:, :], 0.0, waits=W("s_sb"))
            c_msS = vops("memset", PS[5][:, :], 0.0)
        c_yfm = ST["yfm_rd"][sl]
        for tt in range(NT):
            qsl = tt % 2
            qk = qktm[:, qsl * 2048:(qsl + 1) * 2048]
            for fs in range(6):
                ps = PS[fs % 2]
                pfree = ST["psA"] if fs % 2 == 0 else ST["psB"]
                for et in range(NE):
                    w = []
                    if et == 0:
                        w = W(pfree)
                        if fs == 0 and tt == 0:
                            w += W(c_yfm)
                            if i == 0:
                                w.append((sGW, c_wqkv))
                                w.append((s_v, c_wfold))
                                w.append((s_v, c_ones1))
                    c_q = mm(ps[:, :],
                             yfm[:, sl * 4096 + et * 512 + tt * 128: sl * 4096 + et * 512 + tt * 128 + 128],
                             wqkv_sb[:, et * 3072 + fs * 512: et * 3072 + fs * 512 + 512],
                             start=(et == 0), stop=(et == NE - 1), waits=w,
                             inc=(s_pe if et == NE - 1 else None))
                if fs < 4:
                    wqk = W(ST["qktm"][qsl]) if fs == 0 else []
                    c_ev = vop("copy", qk[:, fs * 512:(fs + 1) * 512], ps[:, :],
                               eng="scalar", waits=[(s_pe, c_q)] + wqk, inc=s_sc)
                    if fs == 1:
                        c_evq = c_ev
                    if fs == 3:
                        c_evk = c_ev
                else:
                    dst = vaug[:, tt * 1040 + (fs - 4) * 520: tt * 1040 + (fs - 4) * 520 + 520] \
                        .rearrange("p (h c) -> p h c", c=65)[:, :, 0:64]
                    w = [(s_pe, c_q)]
                    if i == 0 and tt == 0 and fs == 4:
                        w.append((s_v, c_ones))
                    w += W(ST["vaug"]) if fs == 4 else []
                    c_ev = vop("copy", dst,
                               ps[:, :].rearrange("p (h c) -> p h c", c=64),
                               eng="scalar", waits=w, inc=s_sc)
                if fs % 2 == 0:
                    ST["psA"] = (s_sc, c_ev)
                else:
                    ST["psB"] = (s_sc, c_ev)
            c_qkev = s_sc.n
            lt = lt0 + tt
            for (src_off, dst) in ((0, qkr), (1024, kkr)):
                sv = qk[:, src_off:src_off + 1024].rearrange("p (h z) -> p h z", z=64)
                dv = dst[:, tt * 1024:(tt + 1) * 1024].rearrange("p (h z) -> p h z", z=64)
                Ein = sv[:, :, 0:32]
                Oin = sv[:, :, 32:64]
                Eo = dv[:, :, 0:32]
                Oo = dv[:, :, 32:64]
                t1 = rsc[:, 0:512].rearrange("p (h i) -> p h i", i=32)
                def tb(t):
                    return t[:, lt * 32:(lt + 1) * 32] \
                        .rearrange("p (o i) -> p o i", o=1).broadcast_to([128, 16, 32])
                wrope = [(s_sc, c_evq if src_off == 0 else c_evk)]
                if tt == 0 and src_off == 0:
                    wrope += W("rope_free", "rope_free2") + [(sWN, 0)]
                    if i == 0:
                        wrope.append((sCON, c_const))
                vop("tensor_tensor", t1, Oin, tb(tab_se), ALU.mult, waits=wrope)
                vop("tensor_tensor", Eo, Ein, tb(tab_ce), ALU.mult)
                vop("tensor_tensor", Eo, Eo, t1, ALU.subtract)
                vop("tensor_tensor", t1, Ein, tb(tab_so), ALU.mult)
                vop("tensor_tensor", Oo, Oin, tb(tab_co), ALU.mult)
                c_rope = vop("tensor_tensor", Oo, Oo, t1, ALU.add, inc=s_v)
                dfl = dst[:, tt * 1024:(tt + 1) * 1024]
                er = rsc[:, 512:1536]
                c_min = vop("tensor_scalar", er, dfl, 0.0, None, ALU.min, inc=s_v)
                c_elu_exp = vop("activation", er, er, AF.Exp, eng="scalar",
                                waits=[(s_v, c_min)], inc=s_sc)
                c_elu = vop("scalar_tensor_tensor", dfl, dfl, 0.0, er,
                            ALU.max, ALU.add, waits=[(s_sc, c_elu_exp)], inc=s_v)
            ST["qktm"][qsl] = (s_v, c_elu)
            cwq = S.dma("sync", qrt_s[sl, tt * 128:(tt + 1) * 128, :],
                        qkr[:, tt * 1024:(tt + 1) * 1024], sQW[sl],
                        waits=[(s_v, c_elu)] + W(ST["qrt"][sl] if tt == 0 else None))
            cwk = S.dma("sync", krt_s[sl, tt * 128:(tt + 1) * 128, :],
                        kkr[:, tt * 1024:(tt + 1) * 1024], sQW[sl],
                        waits=W(ST["krt"][sl] if tt == 0 else None))
            if i == 0 and tt == 0:
                tap("qkv0", qk[:, 0:2048], [(s_sc, c_qkev)])
                tap("qrope0", qkr[:, 0:1024], [(s_v, c_elu)])
        c_qk_w = sQW[sl].n
        ST["rope_free2"] = (sQW[sl], c_qk_w)
        for hp in range(8):
            w = [(sQW[sl], c_qk_w)] if hp == 0 else []
            c_qfm = S.dmat("scalar", qfm[:, hp * 512:(hp + 1) * 512],
                           qrt_s[sl][:, hp * 128:(hp + 1) * 128], sTa,
                           waits=w + (W("qfm") if hp == 0 else []))
        for hp in range(8):
            w = [(sQW[sl], c_qk_w)] if hp == 0 else []
            c_kfm = S.dmat("sync", kfm[:, hp * 512:(hp + 1) * 512],
                           krt_s[sl][:, hp * 128:(hp + 1) * 128], sTa,
                           waits=w + (W("kfm") if hp == 0 else []))
        ST["qfm"] = None
        ST["kfm"] = None
        qrt_reads = (sTa, c_kfm)
        ST["qrt"][sl] = qrt_reads
        ST["krt"][sl] = qrt_reads
        for c in range(NT):
            first = (chh == 0 and c == 0)
            for r in range(2):
                for hh in range(8):
                    h = r * 8 + hh
                    hp, hl = h // 2, h % 2
                    qsl_ = qfm[:, hp * 512 + c * 128: hp * 512 + c * 128 + 128]
                    ksl_ = kfm[:, hp * 512 + c * 128: hp * 512 + c * 128 + 128]
                    qh = qsl_[64 * hl:64 * hl + 64, :]
                    kh = ksl_[64 * hl:64 * hl + 64, :]
                    pa = PS[2 + (h % 2)]
                    w = W(ST["atb"][h % 2])
                    if h == 0 and c == 0:
                        w += [(sTa, c_qfm), (sTa, c_kfm)]
                    cA = mm(pa[:, 0:128], kh, qh, start=True, stop=True,
                            waits=w, inc=s_pe)
                    wm = [(s_pe, cA)] + W(ST["atm"][h % 4])
                    if i == 0 and c == 0 and h == 0:
                        wm.append((sCON, c_const))
                    cM = vop("tensor_tensor", atm[:, (h % 4) * 128:(h % 4) * 128 + 128],
                             pa[:, 0:128], trim[:, :], ALU.mult,
                             waits=wm, inc=s_v)
                    ST["atb"][h % 2] = (s_v, cM)
                    w = [(s_v, cM)]
                    if hh == 0:
                        w += W(ST["psn"])
                        if c == 0:
                            w.append((s_sc, c_qkev))
                    nreg = PSN[:, 128 * hh: 128 * hh + 65]
                    mm(nreg, atm[:, (h % 4) * 128:(h % 4) * 128 + 128],
                       vaug[:, c * 1040 + 65 * h: c * 1040 + 65 * h + 65],
                       start=True, stop=first, waits=w)
                    cI = s_pe.n
                    if not first:
                        cI = mm(nreg, qh,
                                s_sb[64 * hl:64 * hl + 64, 66 * (h // 2): 66 * (h // 2) + 65],
                                start=False, stop=True,
                                waits=W(ST["s_sb"]) if hh == 0 and r == 0 else (),
                                inc=s_pe)
                    ksl2 = kkr[:, c * 1024 + 64 * h: c * 1024 + 64 * h + 64]
                    wS = []
                    if h == 0:
                        wS = W(ST["s_sb"])
                        if first:
                            wS = wS + [(s_v, c_msS)]
                    mm(PS[4][64 * hl:64 * hl + 64, 64 * (h // 2): 64 * (h // 2) + 64],
                       ksl2, vaug[:, c * 1040 + 65 * h: c * 1040 + 65 * h + 64],
                       start=False, stop=True, tp=(0, 64 * hl),
                       skip_group_check=True, waits=wS)
                    c_supd = mm(PS[5][64 * hl:64 * hl + 64, (h // 2):(h // 2) + 1],
                                ksl2, onesb[:, 0:1],
                                start=False, stop=True,
                                skip_group_check=True,
                                tp=(0, 64 * hl), inc=s_pe)
                    ST["atm"][h % 4] = (s_pe, c_supd)
                den = PSN[:, 0:1024].rearrange("p (h s) -> p h s", s=128)[:, :, 64:65]
                vops("tensor_scalar",
                     ssc[:, 0:8].rearrange("p (h o) -> p h o", o=1),
                     den, 1e-6, None, ALU.max,
                     waits=[(s_pe, s_pe.n)])
                vops("reciprocal", ssc[:, 8:16], ssc[:, 0:8])
                dstv = attn_tm[:, c * 1024 + r * 512: c * 1024 + r * 512 + 512] \
                    .rearrange("p (h z) -> p h z", z=64)
                inv = PSN[:, 0:1024].rearrange("p (h s) -> p h s", s=128)[:, :, 0:64]
                rb = ssc[:, 8:16].rearrange("p (h o) -> p h o", o=1) \
                    .broadcast_to([128, 8, 64])
                wdv = wsmall()
                if c < 2 and r == 0:
                    wdv += [(sXH[0], 0), (sXH[1], 0)]
                cDiv = vop("tensor_tensor", dstv, inv, rb, ALU.mult,
                           waits=wdv, inc=s_v)
                ST["psn"] = (s_v, cDiv)
            w = [(s_pe, c_supd)]
            vop("copy",
                s_sb.rearrange("p (g c) -> p g c", c=66)[:, :, 0:64],
                PS[4][:, :].rearrange("p (g z) -> p g z", z=64),
                eng="scalar", waits=w)
            c_sevac = vop("copy",
                          s_sb.rearrange("p (g c) -> p g c", c=66)[:, :, 64:65],
                          PS[5][:, 0:8].rearrange("p (g o) -> p g o", o=1),
                          eng="scalar", inc=s_sc)
            ST["s_sb"] = (s_sc, c_sevac)
            c_aw = S.dma("sync", art_s[sl, c * 128:(c + 1) * 128, :],
                         attn_tm[:, c * 1024:(c + 1) * 1024], sAW[sl],
                         waits=[(s_v, cDiv)] + (W(ST["art"][sl]) if c == 0 else []))
            if c == 2 and i + 1 < NCHT:
                emit_ln(i + 1)
        c_attn_done = s_pe.n
        ST["qfm"] = (s_pe, c_attn_done)
        ST["kfm"] = (s_pe, c_attn_done)
        ST["vaug"] = (s_pe, c_attn_done)
        ST["rope_free"] = (s_pe, c_attn_done)
        ST["art"][sl] = None
        c_art_w = sAW[sl].n
        if i == 0:
            tap("attn0", attn_tm[:, 0:1024], [(s_v, cDiv)])
        for et in range(NE):
            w = [(sAW[sl], c_art_w)] if et == 0 else []
            c_afm = S.dmat("sync", attnfm[:, et * 512:(et + 1) * 512],
                           art_s[sl][:, et * 128:(et + 1) * 128], sTa,
                           waits=w + (W("attnfm") if et == 0 else []))
        ST["art"][sl] = (sTa, c_afm)
        ST["attnfm"] = None
        for tt in range(NT):
            ev = evst if tt % 2 == 0 else evst2
            for half in range(2):
                ps = PS[half]
                pfree = ST["psA"] if half == 0 else ST["psB"]
                for et in range(NE):
                    w = []
                    if et == 0:
                        w = W(pfree)
                        if tt == 0 and half == 0:
                            w.append((sTa, c_afm))
                            if i == 0:
                                w.append((sGW, c_wout))
                    c_wo = mm(ps[:, :],
                              attnfm[:, et * 512 + tt * 128: et * 512 + tt * 128 + 128],
                              wout_sb[:, et * 1024 + half * 512: et * 1024 + half * 512 + 512],
                              start=(et == 0), stop=(et == NE - 1), waits=w,
                              inc=(s_pe if et == NE - 1 else None))
                yslice = ybk[:, sl * 4096 + tt * 1024 + half * 512: sl * 4096 + tt * 1024 + half * 512 + 512]
                wev = [(s_pe, c_wo)] + W(ST["evst"][tt % 2])
                if tt == 1 and half == 0:
                    wev.append((sAW[sl], c_art_w))
                c_ev = vop("scalar_tensor_tensor",
                           ev[:, half * 512:(half + 1) * 512], ps[:, :], 1.0,
                           yslice, ALU.mult, ALU.add,
                           waits=wev, inc=s_v)
                if half == 0:
                    ST["psA"] = (s_v, c_ev)
                else:
                    ST["psB"] = (s_v, c_ev)
            c_x1aw = S.dma("sync", x1a_s[bb, chh * CH + tt * 128: chh * CH + (tt + 1) * 128, :],
                           ev[:, :], sXH[bb], waits=[(s_v, c_ev)])
            c_x1a_w[(bb, chh, tt)] = c_x1aw
            vops("bn_stats", stat[:, 32:38], ev[:, 0:512], waits=[(s_v, c_ev)])
            vops("bn_stats", stat[:, 38:44], ev[:, 512:1024])
            g3 = stat[:, 32:44].rearrange("p (g c) -> p g c", c=3)
            means = g3[:, :, 1]
            m2s = g3[:, :, 2]
            msq = stat[:, 56:57]
            vops("tensor_tensor", stat[:, 44:48], means, means, ALU.mult)
            vops("reduce_sum", msq, stat[:, 44:48], AXX)
            vops("tensor_scalar", msq, msq, 256.0, None, ALU.mult)
            vops("reduce_sum", stat[:, 57:58], m2s, AXX)
            vops("tensor_tensor", msq, msq, stat[:, 57:58], ALU.add)
            vops("tensor_scalar", msq, msq, 1.0 / 1024.0, None, ALU.mult)
            rinv2 = stat[:, 58:59]
            newton_rsqrt(rinv2, msq, 1e-6, 1)
            vops("tensor_scalar", rinv2, rinv2, 1e-6, 1e6, ALU.max, ALU.min)
            hs = h2c[:, (tt % 2) * 1024:(tt % 2) * 1024 + 1024]
            c_h2 = vop("tensor_scalar", hs, ev[:, :], rinv2, None, ALU.mult,
                       waits=W(ST["h2c"][tt % 2]) + wsmall(), inc=s_v)
            ST["evst"][tt % 2] = (s_v, c_h2)
            c_hw = S.dma("sync", h2_s[bb, chh * CH + tt * 128: chh * CH + (tt + 1) * 128, :],
                         hs, sXH[bb], waits=[(s_v, c_h2)])
            ST["h2c"][tt % 2] = (sXH[bb], c_hw)
            c_h2_w[(bb, chh, tt)] = c_hw
            if i == 0 and tt == 0:
                tap("x1a0", ev[:, :], [(s_v, c_ev)])
                tap("h20", hs, [(s_v, c_h2)])
        ST["ybk"][sl] = (s_v, s_v.n)
        ST["yfm"][sl] = (s_pe, c_attn_done)
        emit_wq_jobs(8, rsc[:, 0:512], rsc[:, 512:1024], rsc[:, 1024:1536], wqa)
    c_p1_done_v = s_v.n
    c_p1_done_pe = s_pe.n
    c_p1_done_sc = s_sc.n


    AL.off = P2_BASE
    h2fm = AL.bf(NE * 2048)
    h2fmB = AL.bf(NE * 2048)
    dacc = AL.bf(16 * 1024)
    silu_g = AL.bf(2 * 1024)
    hidg = AL.bf(9 * 1024)
    hsq = AL.bf(2 * 1024)
    wgsl = AL.bf(2 * 1024)
    wusl = AL.bf(2 * 1024)
    wdg = AL.bf(9 * 1024)
    x1ar = AL.f32(2 * 1024)
    x0r = AL.f32(2 * 1024)
    xst = AL.f32(2 * 1024)
    qsc2 = AL.bf(3 * 512)
    wqxT2 = AL.bf(512)
    assert AL.off <= ARENA_COLS, AL.off
    h2b_ = [h2fm, h2fmB]

    CBAR = [(s_pe, c_p1_done_pe), (s_v, c_p1_done_v), (s_sc, c_p1_done_sc)]
    BAR = (CBAR +
           [(q, q.n) for q in (sX0, sYW[0], sYW[1], sQW[0], sQW[1], sAW[0],
                               sAW[1], sXH[0], sXH[1], sTa, sWN, sGC[0],
                               sGC[1], sGW)])

    sig_done = None
    c_h2f_b = [None, None]
    for b in range(B):
        for et in range(NE):
            w = ([(sXH[b], c_h2_w[(b, NCH - 1, NT - 1)])] + CBAR) \
                if et == 0 else []
            c_h2f_b[b] = S.dmat("scalar", h2b_[b][:, et * 2048:(et + 1) * 2048],
                                h2_s[b][:, et * 128:(et + 1) * 128], sTb[b],
                                waits=w)
    for b in range(B):
        for et in range(NE):
            w = [(sTb[b], c_h2f_b[b])] if et == 0 else []
            vops("reduce_sum", ssc[:, 16 + et:17 + et],
                 h2b_[b][:, et * 2048:(et + 1) * 2048], AXX, waits=w)
        vops("tensor_scalar", ssc[:, 16:24], ssc[:, 16:24], 1.0 / 2048.0, None, ALU.mult)
        c_xm = vops("tensor_copy", sdpb[:, 8 * b:8 * b + 8], ssc[:, 16:24])
        dA = 520 * b
        for et in range(NE):
            c_dp1 = mm(PS[3][0:1, 0:256], sdpb[:, 8 * b + et:8 * b + et + 1],
                       wdp1_sb[:, et * 256:(et + 1) * 256],
                       start=(et == 0), stop=(et == NE - 1),
                       waits=([(s_v, c_xm)] + BAR) if et == 0 else (),
                       inc=(s_pe if et == NE - 1 else None))
        c_sil = vop("activation", sdp[0:1, dA:dA + 256], PS[3][0:1, 0:256], AF.Silu,
                    eng="scalar", waits=[(s_pe, c_dp1)] + (BAR if b == 0 else []),
                    inc=s_sc)
        vops("tensor_tensor", sdp[0:1, dA + 256:dA + 512], sdp[0:1, dA:dA + 256],
             wdp2r[0:1, :], ALU.mult, waits=[(s_sc, c_sil)])
        vops("reduce_sum", sdp[0:1, dA + 512:dA + 513], sdp[0:1, dA + 256:dA + 512], AXX)
        c_dp2 = vops("tensor_scalar", sdp[0:1, dA + 513:dA + 514],
                     sdp[0:1, dA + 512:dA + 513], 0.5, None, ALU.mult)
        c_th = vop("activation", sdp[0:1, dA + 514:dA + 515],
                   sdp[0:1, dA + 513:dA + 514], AF.Tanh, eng="scalar",
                   waits=[(s_v, c_dp2)], inc=s_sc)
        sig_done = vops("tensor_scalar", sdp[0:1, 1040 + b:1041 + b],
                        sdp[0:1, dA + 514:dA + 515],
                        0.5, 0.5, ALU.mult, ALU.add, waits=[(s_sc, c_th)])
    PF = {}
    for f_ in range(2):
        sl2 = f_ % 2
        w0 = list(CBAR) if f_ == 0 else []
        cwg_p = S.dma("gpsimd", wgsl[:, sl2 * 1024:(sl2 + 1) * 1024]
                      .rearrange("p (et k) -> p et k", k=128),
                      wga_in[:, f_ * 128:(f_ + 1) * 128]
                      .rearrange("(et p) k -> p et k", p=128),
                      sGA[sl2], waits=w0)
        cwu_p = S.dma("gpsimd", wusl[:, sl2 * 1024:(sl2 + 1) * 1024]
                      .rearrange("p (et k) -> p et k", k=128),
                      wua_in[:, f_ * 128:(f_ + 1) * 128]
                      .rearrange("(et p) k -> p et k", p=128), sGA[sl2])
        cwd_p = S.dma("gpsimd", wdg[:, f_ * 1024:(f_ + 1) * 1024],
                      wda_in[f_ * 128:(f_ + 1) * 128, :], sGA[sl2])
        PF[f_] = (cwg_p, cwu_p, cwd_p)

    vops("memset", sdp[0:1, 1056:1064], 0.0)
    c_st = vops("tensor_tensor", sdp[0:1, 1056:1057], sdp[0:1, 1040:1041],
                sdp[0:1, 1041:1042], ALU.add)
    c_cc_in = S.dma("gpsimd", cc_in[:, :], sdp[0:1, 1056:1064], sGX,
                    waits=[(s_v, c_st)])
    def _ccfn(e):
        e.collective_compute(
            "AllReduce", ALU.add, replica_groups=[list(range(8))],
            ins=[cc_in.ap().opt()], outs=[cc_out.ap().opt()]).then_inc(s_cc.h)
        return None
    S.op("gpsimd", _ccfn, waits=[(sGX, c_cc_in)])
    s_cc.n = 1
    c_cc_rd = S.dma("gpsimd", sdp[0:1, 1072:1080], cc_out[:, :], sGX,
                    waits=[(s_cc, 1)])
    vops("tensor_scalar", sdp[0:1, 1088:1089], sdp[0:1, 1072:1073], 192.0, 1536.0,
         ALU.mult, ALU.add, waits=[(sGX, c_cc_rd)])
    c_ym1 = vops("tensor_scalar", sdp[0:1, 1089:1090], sdp[0:1, 1088:1089], 1.0, None,
                 ALU.subtract)
    c_bc = mm(PS[3][0:128, 256:257], ones1f[0:1, :], sdp[0:1, 1089:1090],
              start=True, stop=True, waits=[(s_v, c_ym1)], inc=s_pe)
    c_bc2 = vops("tensor_copy", ssc[:, 24:25], PS[3][0:128, 256:257],
                 waits=[(s_pe, c_bc)])
    vops("tensor_single_scalar", mgc[:, :], iotac[:, :], ssc[:, 24:25], ALU.is_le)
    c_mg = vops("tensor_copy", mgcb[:, :], mgc[:, :])
    vops("reduce_sum", ssc[:, 30:31], mgc[:, :], AXX)
    c_ms = vops("tensor_copy", sdpb[:, 8:9], ssc[:, 30:31])
    c_sz = mm(PS[3][0:1, 300:301], sdpb[:, 8:9], onesb[:, 0:1], start=True,
              stop=True, waits=[(s_v, c_ms)], inc=s_pe)
    c_sz2 = vops("tensor_copy", sdp[0:1, 1090:1091], PS[3][0:1, 300:301],
                 waits=[(s_pe, c_sz)])
    c_rsz = vops("reciprocal", sdp[0:1, 1091:1092], sdp[0:1, 1090:1091])
    c_rb = mm(PS[3][0:128, 257:258], ones1f[0:1, :], sdp[0:1, 1091:1092],
              start=True, stop=True, waits=[(s_v, c_rsz)], inc=s_pe)
    c_rsbc = vops("tensor_copy", ssc[:, 25:26], PS[3][0:128, 257:258],
                  waits=[(s_pe, c_rb)])
    tap("sizes", sdp[0:1, 1040:1072], [(s_v, c_rsbc)])

    def emit_wq_job_p2():
        p = wq_state["pend"]
        if p is not None:
            wq_state["pend"] = None
            (xin, w_, dstT, ti, ko, cq, sl2) = p
            nsub = w_ // 128
            c_T = None
            for t in range(nsub):
                w = []
                if t == 0:
                    w = [(s_v, cq)] + W(wq_state["ps3"])
                    if wq_state["p2first_t"]:
                        w.append((s_v, c_rsbc))
                c_T = mm(PS[3][:, t * 128:(t + 1) * 128],
                         xin[:, t * 128:(t + 1) * 128], idb[:, :],
                         start=True, stop=True, waits=w,
                         inc=(s_pe if t == nsub - 1 else None))
            wq_state["free"][sl2] = (s_pe, c_T)
            wev = [(s_pe, c_T)] + W(wq_state["t2free"])
            if wq_state["p2first_t"]:
                wev += [(sXH[0], 0), (sXH[1], 0)]
                wq_state["p2first_t"] = False
            c_evT = vop("copy", wqxT2[:, 0:w_], PS[3][:, 0:w_], eng="scalar",
                        waits=wev, inc=s_sc)
            wq_state["ps3"] = (s_sc, c_evT)
            cw = S.dma("sync",
                       dstT[ko:ko + w_, ti * 128:(ti + 1) * 128]
                       .rearrange("(t p) n -> p t n", p=128),
                       wqxT2[:, 0:w_].rearrange("p (t n) -> p t n", n=128),
                       sWN, waits=[(s_sc, c_evT)])
            wq_state["t2free"] = (sWN, cw)
        i = wq_state["i"]
        if i >= len(wq_jobs):
            return
        wq_state["i"] = i + 1
        src, ti, ko, w_, dstT = wq_jobs[i]
        sl2 = i % 2
        xin = wqx[:, sl2 * 512: sl2 * 512 + w_]
        cL = S.dma("gpsimd", xin, src[ti * 128:(ti + 1) * 128, ko:ko + w_],
                   sGC[sl2], waits=W(wq_state["free"][sl2]))
        wq2 = []
        if wq_state["p2first_q"]:
            wq2 = [(s_pe, c_p1_done_pe), (sXH[0], 0), (sXH[1], 0)]
            wq_state["p2first_q"] = False
        cq = quant_ops(xin, w_, qsc2[:, 0:512], qsc2[:, 512:1024],
                       qsc2[:, 1024:1536], wqa,
                       waits=[(sGC[sl2], cL)] + wq2)
        wq_state["pend"] = (xin, w_, dstT, ti, ko, cq, sl2)

    zqfm = h2fm
    zx2 = arena[:, 0:4096].bitcast(F32)
    zzst = arena[:, 4096:5120]
    za_e = arena[:, 5120:6144]
    zc1_e = arena[:, 96256:97280]
    zt7_e = arena[:, 97280:98304]
    zan_e = arena[:, 98304:99328]
    ST5 = {"x1r": [None, None], "zst": None, "first": True, "rt_first": True}
    c_zqf_t_b = [{}, {}]
    c_x1_w = {}

    def emit_ln2_pair_early(b_, t0):
        first = ST5["first"]
        ST5["first"] = False
        x1vs = []
        for k in range(2):
            tt = t0 + k
            rs_ = tt % 2
            x1v = zx2[:, rs_ * 1024:(rs_ + 1) * 1024]
            x1vs.append(x1v)
            wx = [(sXW, c_x1_w[(b_, tt // 8, tt % 8)])] + W(ST5["x1r"][rs_])
            if first:
                wx += [(s_v, c_p1_done_v), (s_pe, c_rb)]
            c_xr = S.dma("sync", x1v, x1_s[b_, tt * 128:(tt + 1) * 128, :],
                         sRR[rs_], waits=wx)
            w0 = [(sRR[rs_], c_xr)]
            vops("bn_stats", stat[:, k * 12: k * 12 + 6], x1v[:, 0:512],
                 waits=w0)
            vops("bn_stats", stat[:, k * 12 + 6: k * 12 + 12], x1v[:, 512:1024])
        yield
        g2v = stat[:, 0:24].rearrange("p (t g c) -> p t g c", g=4, c=3)
        means = g2v[:, :, :, 1]
        m2s = g2v[:, :, :, 2]
        mt = stat[:, 32:34]
        vt = stat[:, 34:36]
        vops("reduce_sum", mt.rearrange("p (t o) -> p t o", o=1), means, AXX)
        vops("tensor_scalar", mt, mt, 0.25, None, ALU.mult)
        vops("reduce_sum", vt.rearrange("p (t o) -> p t o", o=1), m2s, AXX)
        vops("tensor_tensor", stat[:, 36:38], mt, mt, ALU.mult)
        t8 = stat[:, 112:120].rearrange("p (t g) -> p t g", g=4)
        vops("tensor_tensor", t8, means, means, ALU.mult)
        vops("reduce_sum", stat[:, 38:40].rearrange("p (t o) -> p t o", o=1),
             t8, AXX)
        vops("tensor_scalar", stat[:, 38:40], stat[:, 38:40], 256.0, None,
             ALU.mult)
        vops("tensor_tensor", vt, vt, stat[:, 38:40], ALU.add)
        vops("tensor_scalar", vt, vt, 1.0 / 1024.0, None, ALU.mult)
        vops("tensor_tensor", vt, vt, stat[:, 36:38], ALU.subtract)
        a1 = stat[:, 156:158]
        newton_rsqrt(a1, vt, 1e-5, 2)
        yield
        for k in range(2):
            if k == 1:
                yield
            tt = t0 + k
            rs_ = tt % 2
            c_z = vop("tensor_scalar", zzst[:, :], x1vs[k], mt[:, k:k + 1],
                      a1[:, k:k + 1], ALU.subtract, ALU.mult,
                      waits=W(ST5["zst"]) + wsmall(), inc=s_v)
            ST5["x1r"][rs_] = (s_v, c_z)
            wq_ = []
            if first and k == 0:
                wq_ = [(sXH[0], 0), (sXH[1], 0)]
            cq = quant_ops(zzst[:, :], 1024, za_e, zc1_e, zt7_e, zan_e,
                           waits=wq_,
                           sc_waits=[(s_pe, c_rb)] if first and k == 0 else ())
            cw = S.dma("sync", zq_b[b_][tt * 128:(tt + 1) * 128, :], zzst[:, :],
                       sWQ[b_], waits=[(s_v, cq)])
            ST5["zst"] = (sWQ[b_], cw)
            if b_ == 0 and tt == 0:
                tap("zq0", zzst[:, :], [(s_v, cq)])
            if b_ == 0 and tt % 4 == 3:
                t0_ = tt - 3
                c_zqf = None
                for et in range(NE):
                    w = [(sWQ[0], cw)] if et == 0 else []
                    if et == 0 and ST5["rt_first"]:
                        w.append((s_pe, c_p2a_b0_pe[0]))
                        ST5["rt_first"] = False
                    c_zqf = S.dmat("scalar",
                                   zqfm[:, et * 2048 + t0_ * 128:
                                        et * 2048 + (tt + 1) * 128],
                                   zq_b[0][t0_ * 128:(tt + 1) * 128,
                                           et * 128:(et + 1) * 128],
                                   sTb[0], waits=w)
                c_zqf_t_b[0][tt // 4] = c_zqf

    ST3 = {"wg": [None, None], "wdg": None,
           "sg": [None, None], "hsq": [None, None], "hidg": None,
           "pg": None, "pu": None, "pd": [None, None], "x1r": [None, None],
           "x0r": [None, None], "xst": [None, None], "ps2": None}
    c_p2a_b0_pe = [None]
    PD0 = {"gen": None, "n": 0}

    def pair_drv(PD, b_, start_now, limit):
        g_ = PD["gen"]
        if g_ is not None:
            if next(g_, "end") == "end":
                PD["gen"] = None
            return
        if start_now and PD["n"] < limit:
            PD["gen"] = emit_ln2_pair_early(b_, 2 * PD["n"])
            PD["n"] += 1
            next(PD["gen"])
    for b in range(B):
        h2f = h2b_[b]
        for et in range(NE):
            w = [(sTb[b], c_h2f_b[b])] if et == 0 else []
            c_g2 = vop("tensor_scalar", h2f[:, et * 2048:(et + 1) * 2048],
                       h2f[:, et * 2048:(et + 1) * 2048], g2c[:, et:et + 1], None,
                       ALU.mult, waits=w, inc=s_v)
        c_msQ = vops("memset", PS[2][:, :], 0.0, waits=W(ST3["ps2"]))
        for th in range(2):
            for fg in range(4):
                for j in range(9):
                    f = fg * 9 + j
                    sl2 = f % 2
                    if b == 0 and th == 0 and fg == 0 and j < 2:
                        cwg, cwu, cwd = PF[f]
                    else:
                        cwg = S.dma("gpsimd", wgsl[:, sl2 * 1024:(sl2 + 1) * 1024]
                                    .rearrange("p (et k) -> p et k", k=128),
                                    wga_in[:, f * 128:(f + 1) * 128]
                                    .rearrange("(et p) k -> p et k", p=128),
                                    sGA[sl2], waits=W(ST3["wg"][sl2]))
                        cwu = S.dma("gpsimd", wusl[:, sl2 * 1024:(sl2 + 1) * 1024]
                                    .rearrange("p (et k) -> p et k", k=128),
                                    wua_in[:, f * 128:(f + 1) * 128]
                                    .rearrange("(et p) k -> p et k", p=128), sGA[sl2])
                        cwd = S.dma("gpsimd", wdg[:, j * 1024:(j + 1) * 1024],
                                    wda_in[f * 128:(f + 1) * 128, :], sGA[sl2],
                                    waits=W(ST3["wdg"]) if j == 0 else ())
                    c_gh = vop("tensor_scalar", wdg[:, j * 1024:(j + 1) * 1024],
                               wdg[:, j * 1024:(j + 1) * 1024],
                               ghc[:, f:f + 1], None, ALU.mult,
                               waits=[(sGA[sl2], cwd)] +
                               ([(sCON, c_const)] if b == 0 and th == 0 and f == 0 else []),
                               inc=s_v)
                    for ts in range(2):
                        for et in range(NE):
                            w = []
                            if et == 0 and ts == 0:
                                w = [(sGA[sl2], cwg)] + W(ST3["pg"])
                                if fg == 0 and j == 0 and th == 0:
                                    w.append((s_v, c_g2))
                                    if b == 0:
                                        w.append((s_v, c_p1_done_v))
                            c_gmm = mm(PSN[:, ts * 512:(ts + 1) * 512],
                                       wgsl[:, sl2 * 1024 + et * 128: sl2 * 1024 + et * 128 + 128],
                                       h2f[:, et * 2048 + th * 1024 + ts * 512:
                                           et * 2048 + th * 1024 + (ts + 1) * 512],
                                       start=(et == 0), stop=(et == NE - 1), waits=w,
                                       inc=(s_pe if et == NE - 1 else None))
                    for ts in range(2):
                        for et in range(NE):
                            w = []
                            if et == 0 and ts == 0:
                                w = [(sGA[sl2], cwu)] + W(ST3["pu"])
                                if b == 0 and th == 0 and f == 0:
                                    w.append((s_sc, c_p1_done_sc))
                            c_umm = mm(PS[ts][:, :],
                                       wusl[:, sl2 * 1024 + et * 128: sl2 * 1024 + et * 128 + 128],
                                       h2f[:, et * 2048 + th * 1024 + ts * 512:
                                           et * 2048 + th * 1024 + (ts + 1) * 512],
                                       start=(et == 0), stop=(et == NE - 1), waits=w,
                                       inc=(s_pe if et == NE - 1 else None))
                    c_si = vop("activation",
                               silu_g[:, sl2 * 1024: sl2 * 1024 + 1024],
                               PSN[:, 0:1024], AF.Silu, eng="scalar",
                               waits=[(s_pe, c_gmm)] + W(ST3["sg"][sl2]),
                               inc=s_sc)
                    c_hid = None
                    for ts in range(2):
                        c_hid = vop("scalar_tensor_tensor",
                                    hidg[:, j * 1024 + ts * 512: j * 1024 + (ts + 1) * 512],
                                    silu_g[:, sl2 * 1024 + ts * 512: sl2 * 1024 + (ts + 1) * 512],
                                    mgc[:, f:f + 1], PS[ts][:, :], ALU.mult, ALU.mult,
                                    waits=[(s_sc, c_si), (s_pe, c_umm)] +
                                    ([(s_v, c_mg)] if b == 0 and th == 0 and f == 0 else []) +
                                    ([(sWN, 0)] if b == 0 and th == 0 and fg == 0
                                     and j == 7 and ts == 0 else []) +
                                    (W(ST3["hidg"]) if j == 0 and ts == 0 else []),
                                    inc=s_v)
                    ST3["sg"][sl2] = (s_v, c_hid)
                    ST3["pg"] = (s_sc, c_si)
                    ST3["pu"] = (s_v, c_hid)
                    c_q2 = vop("activation", hsq[:, sl2 * 1024: sl2 * 1024 + 1024],
                               hidg[:, j * 1024:(j + 1) * 1024],
                               AF.Square, eng="scalar",
                               waits=[(s_v, c_hid)] + W(ST3["hsq"][sl2]),
                               inc=s_sc)
                    for tk in range(8):
                        wq_ = [(s_sc, c_q2)] if tk == 0 else []
                        if f == 0 and tk == 0 and th == 0:
                            wq_ += [(s_v, c_msQ)]
                        c_ssq = mm(PS[2][:, th * 8 + tk: th * 8 + tk + 1],
                                   hsq[:, sl2 * 1024 + tk * 128: sl2 * 1024 + tk * 128 + 128],
                                   onesb[:, 0:1], start=False, stop=True,
                                   skip_group_check=True, waits=wq_,
                                   inc=(s_pe if tk == 7 else None))
                    ST3["hsq"][sl2] = (s_pe, c_ssq)
                    ST3["wg"][sl2] = (s_pe, c_umm)
                    if b == 0:
                        emit_wq_job_p2()
                    else:
                        inst1 = (th * 4 + fg) * 9 + j
                        if inst1 % 2 == 0 or PD0["gen"] is None:
                            pair_drv(PD0, 0, inst1 % 9 == 4, 8)
                if fg == 3:
                    vops("tensor_scalar", stat[:, 40:48],
                         PS[2][:, th * 8: th * 8 + 8], ssc[:, 25:26], None,
                         ALU.mult, waits=[(s_pe, c_ssq)])
                    newton_rsqrt(stat[:, 48:56], stat[:, 40:48], 1e-6, 8)
                    c_ri = vops("tensor_scalar", stat[:, 48:56], stat[:, 48:56],
                                1e-6, 1e6, ALU.max, ALU.min)
                    ST3["ps2"] = (s_v, c_ri)
                c_dn = None
                for tk in range(8):
                    if fg == 3:
                        rs_ = tk % 2
                        rows0 = b, th * 1024 + tk * 128
                        c_x1r = S.dma("sync", x1ar[:, rs_ * 1024:(rs_ + 1) * 1024],
                                      x1a_s[b, th * 1024 + tk * 128: th * 1024 + (tk + 1) * 128, :],
                                      sRR[rs_], waits=W(ST3["x1r"][rs_]) +
                                      [(sXH[b], sXH[b].n)])
                        c_x0r = S.dma("sync", x0r[:, rs_ * 1024:(rs_ + 1) * 1024],
                                      x_in[b, th * 1024 + tk * 128: th * 1024 + (tk + 1) * 128, :],
                                      sRR[rs_], waits=W(ST3["x0r"][rs_]) +
                                      ([(sAW[0], 0), (sAW[1], 0)]
                                       if b == 0 and th == 0 and tk < 2 else []))
                    for j in range(9):
                        for eh in range(2):
                            w = []
                            if j == 0 and eh == 0:
                                w = W(ST3["pd"][0], ST3["pd"][1])
                                if tk == 0:
                                    w += [(s_v, c_gh)]
                            c_dn = mm(PS[4 + eh][:, :],
                                      hidg[:, j * 1024 + tk * 128: j * 1024 + tk * 128 + 128],
                                      wdg[:, j * 1024 + eh * 512: j * 1024 + (eh + 1) * 512],
                                      start=(j == 0), stop=(j == 8), waits=w,
                                      inc=(s_pe if (j == 8) else None))
                    if fg < 3:
                        for eh in range(2):
                            if fg == 0:
                                c_ev = vop("tensor_copy",
                                           dacc[:, (th * 8 + tk) * 1024 + eh * 512:
                                                (th * 8 + tk) * 1024 + (eh + 1) * 512],
                                           PS[4 + eh][:, :],
                                           waits=[(s_pe, c_dn)] if eh == 0 else (),
                                           inc=s_v)
                            else:
                                c_ev = vop("scalar_tensor_tensor",
                                           dacc[:, (th * 8 + tk) * 1024 + eh * 512:
                                                (th * 8 + tk) * 1024 + (eh + 1) * 512],
                                           PS[4 + eh][:, :], 1.0,
                                           dacc[:, (th * 8 + tk) * 1024 + eh * 512:
                                                (th * 8 + tk) * 1024 + (eh + 1) * 512],
                                           ALU.mult, ALU.add,
                                           waits=[(s_pe, c_dn)] if eh == 0 else (),
                                           inc=s_v)
                            ST3["pd"][eh] = (s_v, c_ev)
                    else:
                        rs_ = tk % 2
                        xs_ = xst[:, rs_ * 1024:(rs_ + 1) * 1024]
                        for eh in range(2):
                            w = [(s_pe, c_dn), (s_v, c_ri), (sRR[rs_], c_x1r),
                                 (sRR[rs_], c_x0r)] if eh == 0 else []
                            w += W(ST3["xst"][rs_]) if eh == 0 else []
                            vop("scalar_tensor_tensor", xs_[:, eh * 512:(eh + 1) * 512],
                                PS[4 + eh][:, :], 1.0,
                                dacc[:, (th * 8 + tk) * 1024 + eh * 512:
                                     (th * 8 + tk) * 1024 + (eh + 1) * 512],
                                ALU.mult, ALU.add, waits=w)
                            vop("tensor_scalar", xs_[:, eh * 512:(eh + 1) * 512],
                                xs_[:, eh * 512:(eh + 1) * 512],
                                stat[:, 48 + tk:49 + tk], None, ALU.mult)
                            vop("tensor_tensor", xs_[:, eh * 512:(eh + 1) * 512],
                                xs_[:, eh * 512:(eh + 1) * 512],
                                x1ar[:, rs_ * 1024 + eh * 512: rs_ * 1024 + (eh + 1) * 512],
                                ALU.add)
                            c_fx = vop("tensor_tensor", xs_[:, eh * 512:(eh + 1) * 512],
                                       xs_[:, eh * 512:(eh + 1) * 512],
                                       x0r[:, rs_ * 1024 + eh * 512: rs_ * 1024 + (eh + 1) * 512],
                                       ALU.add, inc=s_v)
                            ST3["pd"][eh] = (s_v, c_fx)
                        ST3["x1r"][rs_] = (s_v, c_fx)
                        ST3["x0r"][rs_] = (s_v, c_fx)
                        cw = S.dma("sync", x1_s[b, th * 1024 + tk * 128:
                                               th * 1024 + (tk + 1) * 128, :],
                                   xs_, sXW, waits=[(s_v, c_fx)])
                        ST3["xst"][rs_] = (sXW, cw)
                        c_x1_w[(b, th, tk)] = cw
                        if b == 0 and th == 0 and tk == 0:
                            tap("x10", xs_, [(s_v, c_fx)])
                ST3["hidg"] = (s_pe, c_dn)
                ST3["wdg"] = (s_pe, c_dn)
        if b == 0:
            while wq_state["i"] < len(wq_jobs) or wq_state["pend"] is not None:
                emit_wq_job_p2()
            c_p2a_b0_pe[0] = s_pe.n
    while PD0["gen"] is not None or PD0["n"] < 8:
        pair_drv(PD0, 0, True, 8)
    c_wqT_w = sWN.n
    c_p2a_pe = s_pe.n

    AL.off = P2_BASE
    zqfm = AL.bf(8 * 2048)
    wqnr = AL.bf(4 * 4096)
    hbuf = AL.bf(4 * 512)
    qAN = AL.bf(1024)
    qA = AL.bf(1024)
    qC1 = AL.bf(1024)
    qT7 = AL.bf(1024)
    x1r2 = AL.f32(2 * 1024)
    hqg = AL.bf(8 * 2048)
    wqdg = AL.bf(8 * 1024)
    dacc2 = AL.bf(16 * 1024)
    xst2 = AL.f32(2 * 1024)
    assert AL.off <= ARENA_COLS, AL.off

    c_p2a_v = s_v.n
    c_p2a_sc = s_sc.n
    BAR2 = ([(s_pe, c_p2a_pe), (s_v, c_p2a_v), (s_sc, c_p2a_sc)] +
            [(q, q.n) for q in (sRR[0], sRR[1], sXW, sXH[0], sXH[1], sTb[0], sTb[1],
                                sGA[0], sGA[1], sGX, sWN)])
    ST4 = {"x1r": [None, None],
           "wq": [[None, None], [None, None]], "pg": [None, None],
           "pu": [None, None],
           "sgb": [None, None], "hbuf": [None, None], "zqfm": None,
           "hqg": None, "wqdg": None,
           "pd": [[None, None], [None, None]], "xst": [None, None]}
    c_hqf_fg_b = [{}, {}]
    sgb = qsc2[:, 512:1024]
    sgb2 = qsc2[:, 0:512]
    c_out_w = 0

    def gu_gen(b):
        c_gmm = c_umm = None
        for nr in range(6):
            nw = 512 if nr < 5 else 256
            sl2 = nr % 2
            cwg = S.dma("gpsimd",
                        wqnr[:, sl2 * 4096: sl2 * 4096 + 8 * nw]
                        .rearrange("p (et n) -> p et n", n=nw),
                        wqgT_s[:, nr * 512: nr * 512 + nw]
                        .rearrange("(et p) n -> p et n", p=128),
                        sGB[sl2], waits=W(ST4["wq"][0][sl2]) +
                        (([(sWN, c_wqT_w)] + BAR2) if nr == 0 and b == 0 else []))
            cwu = S.dma("gpsimd",
                        wqnr[:, 8192 + sl2 * 4096: 8192 + sl2 * 4096 + 8 * nw]
                        .rearrange("p (et n) -> p et n", n=nw),
                        wquT_s[:, nr * 512: nr * 512 + nw]
                        .rearrange("(et p) n -> p et n", p=128),
                        sGB[sl2], waits=W(ST4["wq"][1][sl2]))
            for tt in range(16):
                par = tt % 2
                first0 = (b == 0 and nr == 0 and tt < 2)
                pg_ap = PSN[:, par * 512: par * 512 + nw]
                pu_ap = PS[par][:, 0:nw]
                sg_ap = (sgb if par == 0 else sgb2)[:, 0:nw]
                hb = hbuf[:, (tt % 4) * 512: (tt % 4) * 512 + nw]
                for et in range(NE):
                    w = []
                    if et == 0:
                        w = W(ST4["pg"][par])
                        if tt == 0:
                            w.append((sGB[sl2], cwg))
                        if nr == 0:
                            w.append((sTb[b], c_zqf_t_b[b][tt // 4]))
                        if first0:
                            w.append((s_sc, c_p2a_sc))
                    c_gmm = mm(pg_ap,
                               zqfm[:, et * 2048 + tt * 128: et * 2048 + tt * 128 + 128],
                               wqnr[:, sl2 * 4096 + et * nw: sl2 * 4096 + (et + 1) * nw],
                               start=(et == 0), stop=(et == NE - 1), waits=w,
                               inc=(s_pe if et == NE - 1 else None))
                for et in range(NE):
                    w = []
                    if et == 0:
                        w = W(ST4["pu"][par])
                        if tt == 0:
                            w.append((sGB[sl2], cwu))
                        if first0:
                            w.append((s_v, c_p2a_v))
                    c_umm = mm(pu_ap,
                               zqfm[:, et * 2048 + tt * 128: et * 2048 + tt * 128 + 128],
                               wqnr[:, 8192 + sl2 * 4096 + et * nw:
                                    8192 + sl2 * 4096 + (et + 1) * nw],
                               start=(et == 0), stop=(et == NE - 1), waits=w,
                               inc=(s_pe if et == NE - 1 else None))
                c_si = vop("activation", sg_ap, pg_ap, AF.Silu,
                           eng="scalar",
                           waits=[(s_pe, c_gmm)] + W(ST4["sgb"][par]) +
                           ([(s_v, c_p2a_v)] if first0 else []), inc=s_sc)
                ST4["pg"][par] = (s_sc, c_si)
                c_h = vop("tensor_tensor", hb, sg_ap, pu_ap, ALU.mult,
                          waits=[(s_sc, c_si), (s_pe, c_umm)] +
                          W(ST4["hbuf"][(tt // 2) % 2]), inc=s_v)
                ST4["pu"][par] = (s_v, c_h)
                ST4["sgb"][par] = (s_v, c_h)
                if nw == 512:
                    if par == 1:
                        xin2 = hbuf[:, (tt - 1) % 4 * 512: ((tt - 1) % 4 + 2) * 512]
                        cq = quant_ops(xin2, 1024, qA, qC1, qT7, qAN,
                                       waits=[(s_v, c_h)])
                        cw = S.dma("sync",
                                   hq_b[b][(tt - 1) * 128:(tt + 1) * 128,
                                           nr * 512: nr * 512 + 512]
                                   .rearrange("(t p) n -> p t n", p=128),
                                   xin2.rearrange("p (t n) -> p t n", n=512),
                                   sWQ[b], waits=[(s_v, cq)])
                        ST4["hbuf"][(tt // 2) % 2] = (sWQ[b], cw)
                else:
                    cq = quant_ops(hb, nw, qA, qC1, qT7, qAN,
                                   waits=[(s_v, c_h)])
                    cw = S.dma("sync",
                               hq_b[b][tt * 128:(tt + 1) * 128,
                                       nr * 512: nr * 512 + nw],
                               hb, sWQ[b], waits=[(s_v, cq)])
                    ST4["hbuf"][(tt // 2) % 2] = (sWQ[b], cw)
                yield
            ST4["wq"][0][sl2] = (s_pe, c_gmm)
            ST4["wq"][1][sl2] = (s_pe, c_umm)
            if nr == 1 and b == 0:
                c_hqf = None
                for j_ in range(8):
                    w = ([(sWQ[b], 0)] if j_ == 0 else []) + \
                        (W(ST4["hqg"]) if j_ == 0 else [])
                    c_hqf = S.dmat("scalar", hqg[:, j_ * 2048:(j_ + 1) * 2048],
                                   hq_b[b][:, j_ * 128:(j_ + 1) * 128], sTb[b],
                                   waits=w)
                c_hqf_fg_b[0][0] = c_hqf
        ST4["zqfm"] = (s_pe, s_pe.n)

    def down_gen(b):
        for fg in range(3):
            gsz = 8 if fg < 2 else 6
            for j in range(gsz):
                ft = fg * 8 + j
                if fg > 0 or b == 1:
                    w = ([(sWQ[b], 0)] if j == 0 else []) + \
                        (W(ST4["hqg"]) if j == 0 else [])
                    c_hqf_fg_b[b][fg] = S.dmat(
                        "scalar", hqg[:, j * 2048:(j + 1) * 2048],
                        hq_b[b][:, ft * 128:(ft + 1) * 128], sTb[b], waits=w)
                c_wd2 = S.dma("gpsimd", wqdg[:, j * 1024:(j + 1) * 1024],
                              wqdT_s[ft * 128:(ft + 1) * 128, :], sGD,
                              waits=W(ST4["wqdg"]) if j == 0 else ())
            for tk in range(16):
                rs_ = tk % 2
                pbk = (PS[4], PS[5]) if rs_ == 0 else (PS[2], PS[3])
                if fg == 2:
                    c_xr2 = S.dma("sync", x1r2[:, rs_ * 1024:(rs_ + 1) * 1024],
                                  x1_s[b, tk * 128:(tk + 1) * 128, :], sRR[rs_],
                                  waits=W(ST4["x1r"][rs_]) +
                                  ([(s_v, c_p2a_v)]
                                   if b == 0 and tk < 2 else []))
                c_dn = None
                for j in range(gsz):
                    for eh in range(2):
                        w = []
                        if j == 0 and eh == 0:
                            w = W(ST4["pd"][rs_][0], ST4["pd"][rs_][1])
                            if tk == 0:
                                w += [(sTb[b], c_hqf_fg_b[b][fg]), (sGD, c_wd2)]
                            if b == 0 and fg == 0 and tk < 2:
                                w += [(s_v, c_p2a_v), (s_sc, c_p2a_sc)]
                        c_dn = mm(pbk[eh][:, :],
                                  hqg[:, j * 2048 + tk * 128: j * 2048 + tk * 128 + 128],
                                  wqdg[:, j * 1024 + eh * 512: j * 1024 + (eh + 1) * 512],
                                  start=(j == 0), stop=(j == gsz - 1), waits=w,
                                  inc=(s_pe if j == gsz - 1 else None))
                if fg < 2:
                    for eh in range(2):
                        if fg == 0:
                            c_ev = vop("tensor_copy",
                                       dacc2[:, tk * 1024 + eh * 512: tk * 1024 + (eh + 1) * 512],
                                       pbk[eh][:, :],
                                       waits=[(s_pe, c_dn)] if eh == 0 else (), inc=s_v)
                        else:
                            c_ev = vop("scalar_tensor_tensor",
                                       dacc2[:, tk * 1024 + eh * 512: tk * 1024 + (eh + 1) * 512],
                                       pbk[eh][:, :], 1.0,
                                       dacc2[:, tk * 1024 + eh * 512: tk * 1024 + (eh + 1) * 512],
                                       ALU.mult, ALU.add,
                                       waits=[(s_pe, c_dn)] if eh == 0 else (), inc=s_v)
                        ST4["pd"][rs_][eh] = (s_v, c_ev)
                else:
                    xs_ = xst2[:, rs_ * 1024:(rs_ + 1) * 1024]
                    for eh in range(2):
                        w = ([(s_pe, c_dn), (sRR[rs_], c_xr2)] + W(ST4["xst"][rs_])) \
                            if eh == 0 else []
                        vop("scalar_tensor_tensor", xs_[:, eh * 512:(eh + 1) * 512],
                            pbk[eh][:, :], 1.0,
                            dacc2[:, tk * 1024 + eh * 512: tk * 1024 + (eh + 1) * 512],
                            ALU.mult, ALU.add, waits=w)
                        c_fx = vop("tensor_tensor", xs_[:, eh * 512:(eh + 1) * 512],
                                   xs_[:, eh * 512:(eh + 1) * 512],
                                   x1r2[:, rs_ * 1024 + eh * 512: rs_ * 1024 + (eh + 1) * 512],
                                   ALU.add, inc=s_v)
                        ST4["pd"][rs_][eh] = (s_v, c_fx)
                    ST4["x1r"][rs_] = (s_v, c_fx)
                    cow = S.dma("sync", out_d[b, tk * 128:(tk + 1) * 128, :],
                                xs_, sOut, waits=[(s_v, c_fx)])
                    ST4["xst"][rs_] = (sOut, cow)
                yield
            ST4["hqg"] = (s_pe, c_dn)
            ST4["wqdg"] = (s_pe, c_dn)

    PD1 = {"gen": None, "n": 0}
    u = 0
    for _ in gu_gen(0):
        if u % 2 == 0 or PD1["gen"] is None:
            pair_drv(PD1, 1, u % 12 == 4, 8)
        u += 1
    while PD1["gen"] is not None or PD1["n"] < 8:
        pair_drv(PD1, 1, True, 8)
    c_zqf = None
    for et in range(NE):
        w = [(sWQ[1], 0), (s_pe, s_pe.n)] if et == 0 else []
        c_zqf = S.dmat("scalar", zqfm[:, et * 2048:(et + 1) * 2048],
                       zq2_s[:, et * 128:(et + 1) * 128], sTb[1], waits=w)
    for g_ in range(4):
        c_zqf_t_b[1][g_] = c_zqf
    g1 = gu_gen(1)
    d0 = down_gen(0)
    alive = True
    while alive:
        alive = False
        for _ in range(2):
            if next(g1, "end") != "end":
                alive = True
        if next(d0, "end") != "end":
            alive = True
    for _ in down_gen(1):
        pass
    S.op("sync", lambda e: e.wait_ge(sOut.h, sOut.n))

    with nc.Block() as block:
        def mk(name):
            def run(eng):
                for (fn, waits, inc, amt) in S.q[name]:
                    for (sw, v) in waits:
                        eng.wait_ge(sw.h, v)
                    inst = fn(eng)
                    if inc is not None and inst is not None:
                        inst.then_inc(inc.h, amt)
            return run
        block.sync(mk("sync"))
        block.gpsimd(mk("gpsimd"))
        block.vector(mk("vector"))
        block.scalar(mk("scalar"))
        block.tensor(mk("tensor"))
    ctx.close()
    return nc


TAP_SPECS = {
    "y0": ([128, 1024], F32), "y1": ([128, 1024], F32),
    "y2": ([128, 1024], F32), "y3": ([128, 1024], F32),
    "qkv0": ([128, 2048], BF16), "qrope0": ([128, 1024], BF16),
    "attn0": ([128, 1024], BF16), "x1a0": ([128, 1024], F32),
    "h20": ([128, 1024], BF16), "sizes": ([1, 32], F32),
    "x10": ([128, 1024], F32), "zq0": ([128, 1024], BF16),
}


def kernel(L0_attn_x, L0_attn_w_qkv, L0_attn_w_out, L0_attn_g1, L0_attn_g2,
           L0_attn_w_dp1, L0_attn_w_dp2, L0_attn_w_gate, L0_attn_w_up,
           L0_attn_w_down, L0_attn_g_hidden, L0_ffn_w_gate, L0_ffn_w_up,
           L0_ffn_w_down):
    x = np.asarray(L0_attn_x, dtype=np.float32)
    wqkv = np.asarray(L0_attn_w_qkv, dtype=np.float32)
    wout = np.asarray(L0_attn_w_out, dtype=np.float32)
    g1 = np.asarray(L0_attn_g1, dtype=np.float32).reshape(1, E)
    g2 = np.asarray(L0_attn_g2, dtype=np.float32).reshape(1, E)
    wdp1 = np.asarray(L0_attn_w_dp1, dtype=np.float32)
    wdp2 = np.asarray(L0_attn_w_dp2, dtype=np.float32).reshape(1, 256)
    wga = np.asarray(L0_attn_w_gate, dtype=np.float32)
    wua = np.asarray(L0_attn_w_up, dtype=np.float32)
    wda = np.asarray(L0_attn_w_down, dtype=np.float32)
    gh = np.asarray(L0_attn_g_hidden, dtype=np.float32).reshape(1, FA)
    wqg = np.asarray(L0_ffn_w_gate, dtype=np.float32)
    wqu = np.asarray(L0_ffn_w_up, dtype=np.float32)
    wqd = np.asarray(L0_ffn_w_down, dtype=np.float32)

    ce, co, se, so, tri, iota_col, ident = host_constants()
    tabs = np.concatenate([ce, co, se, so], axis=0)
    pm = _pm(0)
    perm = np.arange(3 * E)
    for h in range(H):
        for i in range(D):
            perm[64 * h + i] = 64 * h + pm[i]
            perm[E + 64 * h + i] = E + 64 * h + pm[i]
    wqkv_p = np.ascontiguousarray(wqkv[:, perm])

    taps = [(nm,) + TAP_SPECS[nm] for nm in DEBUG_TAPS]
    nc = build_program(taps=taps)
    in_maps = []
    for c in range(8):
        in_maps.append({
            "x": np.ascontiguousarray(x[2 * c:2 * c + 2]),
            "wqkv": wqkv_p, "wout": wout, "g1": g1, "g2": g2,
            "wdp1": wdp1, "wdp2": wdp2, "wga": wga, "wua": wua, "wda": wda,
            "gh": gh, "wqg": wqg, "wqu": wqu, "wqd": wqd,
            "tabs": tabs, "trim": np.asarray(tri),
            "iotac": iota_col, "ident": np.asarray(ident),
        })
    res = run_bass_kernel_spmd(nc, in_maps, core_ids=list(range(8)))
    out = np.concatenate([np.asarray(r["out"]) for r in res.results], axis=0)
    if DEBUG_TAPS:
        kernel.tap_results = res.results
    return out
